# version 02
# baseline (speedup 1.0000x reference)
from contextlib import ExitStack

import numpy as np
import ml_dtypes

import concourse.bass as bass
import concourse.mybir as mybir
from concourse.bass_utils import run_bass_kernel_spmd

F32 = mybir.dt.float32
BF16 = mybir.dt.bfloat16
I32 = mybir.dt.int32
AF = mybir.ActivationFunctionType
ALU = mybir.AluOpType
AX = mybir.AxisListType.X

D = 4096
KD = D // 128
F = 3072
NF = F // 128
DK = 1024
NDK = DK // 128
GI = 32
EPS = 1e-5
SCL = 1.0 / 32.0


def build(T=2048, HT=86, upto=99):
    H = 128 * HT
    NT = T // 128
    NC = T // 512
    NKT = T // 128
    nc = bass.Bass("TRN2", target_bir_lowering=False)

    x_d = nc.dram_tensor("x", [T, D], F32, kind="ExternalInput").ap()
    wqkv_d = nc.dram_tensor("wqkv", [D, F], F32, kind="ExternalInput").ap()
    bqkv_d = nc.dram_tensor("bqkv", [F], F32, kind="ExternalInput").ap()
    wo_d = nc.dram_tensor("wo", [DK, D], F32, kind="ExternalInput").ap()
    bo_d = nc.dram_tensor("bo", [D], F32, kind="ExternalInput").ap()
    qwg_d = nc.dram_tensor("qwg", [D, H], I32, kind="ExternalInput").ap()
    zg_d = nc.dram_tensor("zg", [GI, H], I32, kind="ExternalInput").ap()
    sg_d = nc.dram_tensor("sg", [GI, H], F32, kind="ExternalInput").ap()
    qwu_d = nc.dram_tensor("qwu", [D, H], I32, kind="ExternalInput").ap()
    zu_d = nc.dram_tensor("zu", [GI, H], I32, kind="ExternalInput").ap()
    su_d = nc.dram_tensor("su", [GI, H], F32, kind="ExternalInput").ap()
    qwd_d = nc.dram_tensor("qwd", [H, D], I32, kind="ExternalInput").ap()
    zd_d = nc.dram_tensor("zd", [HT, D], I32, kind="ExternalInput").ap()
    sd_d = nc.dram_tensor("sd", [HT, D], F32, kind="ExternalInput").ap()
    id_d = nc.dram_tensor("ident", [128, 128], BF16, kind="ExternalInput").ap()
    ones_d = nc.dram_tensor("ones", [1, 128], BF16, kind="ExternalInput").ap()
    y_d = nc.dram_tensor("y", [T, D], F32, kind="ExternalOutput").ap()

    qkvT_d = nc.dram_tensor("qkvT_v", [NF, 128, T], BF16, kind="Internal").ap()
    x1_d = nc.dram_tensor("x1_v", [NT, 128, D], F32, kind="Internal").ap()
    hT_d = nc.dram_tensor("hT_v", [HT, 128, T], BF16, kind="Internal").ap()
    fo_d = nc.dram_tensor("fo_v", [NT, 128, D], BF16, kind="Internal").ap()
    nzsg_d = nc.dram_tensor("nzsg_v", [GI, H], BF16, kind="Internal").ap()
    nzsu_d = nc.dram_tensor("nzsu_v", [GI, H], BF16, kind="Internal").ap()
    sgb_d = nc.dram_tensor("sgb_v", [GI, H], BF16, kind="Internal").ap()
    sub_d = nc.dram_tensor("sub_v", [GI, H], BF16, kind="Internal").ap()
    sgf_d = nc.dram_tensor("sgf_v", [HT, 4096], BF16, kind="Internal").ap()
    suf_d = nc.dram_tensor("suf_v", [HT, 4096], BF16, kind="Internal").ap()
    sdb_d = nc.dram_tensor("sdb_v", [HT, D], BF16, kind="Internal").ap()
    nzd_d = nc.dram_tensor("nzd_v", [HT, D], BF16, kind="Internal").ap()

    stack = ExitStack()
    E = stack.enter_context
    with stack:
        id_s = E(nc.sbuf_tensor("id_s", [128, 128], BF16))
        idf_s = E(nc.sbuf_tensor("idf_s", [128, 128], F32))
        ones_s = E(nc.sbuf_tensor("ones_s", [1, 128], BF16))
        bqT_s = E(nc.sbuf_tensor("bqT_s", [128, NF], F32))
        sum_s = E(nc.sbuf_tensor("sum_s", [128, 1], F32))
        ssq_s = E(nc.sbuf_tensor("ssq_s", [128, 1], F32))
        nmean_s = E(nc.sbuf_tensor("nmean_s", [128, 1], F32))
        ssqn_s = E(nc.sbuf_tensor("ssqn_s", [128, 1], F32))
        msq_s = E(nc.sbuf_tensor("msq_s", [128, 1], F32))
        var_s = E(nc.sbuf_tensor("var_s", [128, 1], F32))
        lnv_s = E(nc.sbuf_tensor("lnv_s", [128, 1], F32))
        rstd_s = E(nc.sbuf_tensor("rstd_s", [128, 1], F32))
        nmr_s = E(nc.sbuf_tensor("nmr_s", [128, 1], F32))
        sump_s = E(nc.sbuf_tensor("sump_s", [128, NT * 8], F32))
        ssqp_s = E(nc.sbuf_tensor("ssqp_s", [128, NT * 8], F32))
        rstd4_s = E(nc.sbuf_tensor("rstd4_s", [128, NT], F32))
        nmr4_s = E(nc.sbuf_tensor("nmr4_s", [128, NT], F32))

        bar = E(nc.semaphore("bar"))
        NBAR = [0]

        def barrier(drains=()):
            NBAR[0] += 1
            tgt = 5 * NBAR[0]
            dr = {}
            for (en, sem, val) in drains:
                dr.setdefault(en, []).append((sem, val))
            with nc.Block() as blk:
                for engname in ("tensor", "vector", "scalar", "gpsimd", "sync"):
                    def mk(eng, _t=tgt, _en=engname):
                        for (sem, val) in dr.get(_en, []):
                            eng.wait_ge(sem, val)
                        eng.nop().then_inc(bar, 1)
                        eng.wait_ge(bar, _t)
                    getattr(blk, engname)(mk)

        def ln_chain(scalar, ctr, sem, rstd_out, nmr_out):
            def step(ins):
                ctr[0] += 1
                ins.then_inc(sem, 1)
                scalar.wait_ge(sem, ctr[0])
            step(scalar.mul(nmean_s[:], sum_s[:], -1.0 / D))
            step(scalar.mul(ssqn_s[:], ssq_s[:], 1.0 / D))
            step(scalar.activation(msq_s[:], nmean_s[:], AF.Square))
            step(scalar.activation(var_s[:], msq_s[:], AF.Identity, scale=-1.0,
                                   bias=ssqn_s[:, 0:1]))
            step(scalar.activation(lnv_s[:], var_s[:], AF.Ln, bias=np.float32(EPS)))
            step(scalar.activation(rstd_out, lnv_s[:], AF.Exp, scale=-0.5))
            step(scalar.activation(nmr_out, nmean_s[:], AF.Identity, scale=rstd_out))

        jobs = []
        def chunks(total, size):
            out, off = [], 0
            while off < total:
                out.append((off, min(size, total - off)))
                off += size
            return out
        for (z, s, nz, sb, rows, total) in (
                (zg_d, sg_d, nzsg_d, sgb_d, GI, H),
                (zu_d, su_d, nzsu_d, sub_d, GI, H),
                (zd_d, sd_d, nzd_d, sdb_d, HT, D)):
            for (off, w) in chunks(total, 2816):
                jobs.append((z, s, nz, sb, rows, off, w))
        NJOB = len(jobs)
        dpp = [E(nc.semaphore(f"dpp{i}")) for i in range(2)]
        vpp = E(nc.semaphore("vpp"))
        dppo = [E(nc.semaphore(f"dppo{i}")) for i in range(2)]
        dflat = E(nc.semaphore("dflat"))
        ppS = ExitStack()
        ppz = [ppS.enter_context(nc.sbuf_tensor(f"ppz{i}", [128, 2816], I32)) for i in range(2)]
        pps = [ppS.enter_context(nc.sbuf_tensor(f"pps{i}", [128, 2816], F32)) for i in range(2)]
        ppn = [ppS.enter_context(nc.sbuf_tensor(f"ppn{i}", [128, 2816], BF16)) for i in range(2)]
        ppb = [ppS.enter_context(nc.sbuf_tensor(f"ppb{i}", [128, 2816], BF16)) for i in range(2)]
        ppf = ppS.enter_context(nc.sbuf_tensor("ppf", [GI, H], BF16))
        with ppS, nc.Block() as blk:
            @blk.sync
            def _(sync):
                for i, (z, s, nz, sb, rows, off, w) in enumerate(jobs):
                    if i >= 2:
                        sync.wait_ge(vpp, 2 * (i - 1))
                    sync.dma_start(ppz[i % 2][0:rows, 0:w], z[:, off:off + w]).then_inc(dpp[i % 2], 16)
                    sync.dma_start(pps[i % 2][0:rows, 0:w], s[:, off:off + w]).then_inc(dpp[i % 2], 16)

            @blk.vector
            def _(vector):
                for i, (z, s, nz, sb, rows, off, w) in enumerate(jobs):
                    vector.wait_ge(dpp[i % 2], 32 * (i // 2 + 1))
                    if i >= 2:
                        vector.wait_ge(dppo[i % 2], 32 * (i // 2))
                    vector.scalar_tensor_tensor(
                        ppn[i % 2][0:rows, 0:w], ppz[i % 2][0:rows, 0:w], -1.0, pps[i % 2][0:rows, 0:w],
                        ALU.mult, ALU.mult).then_inc(vpp, 1)
                    vector.tensor_copy(ppb[i % 2][0:rows, 0:w], pps[i % 2][0:rows, 0:w]).then_inc(vpp, 1)

            @blk.scalar
            def _(scalar):
                for i, (z, s, nz, sb, rows, off, w) in enumerate(jobs):
                    scalar.wait_ge(vpp, 2 * (i + 1))
                    scalar.dma_start(nz[:, off:off + w], ppn[i % 2][0:rows, 0:w]).then_inc(dppo[i % 2], 16)
                    scalar.dma_start(sb[:, off:off + w], ppb[i % 2][0:rows, 0:w]).then_inc(dppo[i % 2], 16)
                scalar.wait_ge(dppo[0], 32 * ((NJOB + 1) // 2))
                scalar.wait_ge(dppo[1], 32 * (NJOB // 2))
                scalar.nop().then_inc(vpp, 1)

            @blk.gpsimd
            def _(gpsimd):
                gpsimd.wait_ge(vpp, 2 * NJOB + 1)
                nfl = 0
                for (src, dst) in ((sgb_d, sgf_d), (sub_d, suf_d)):
                    gpsimd.dma_start(ppf[0:GI, :], src[:, :]).then_inc(dflat, 16)
                    nfl += 16
                    gpsimd.wait_ge(dflat, nfl)
                    dap = bass.AP(dst.tensor, 0, [[128, GI], [4096, HT], [1, 128]])
                    gpsimd.dma_start(
                        dap, ppf[0:GI, :].rearrange("p (j c) -> p j c", j=HT)
                    ).then_inc(dflat, 16)
                    nfl += 16
                    gpsimd.wait_ge(dflat, nfl)
        barrier()
        if upto < -4:
            return nc

        xnT_stk = ExitStack()
        xnT = xnT_stk.enter_context(nc.sbuf_tensor("xnT", [128, KD * T], BF16))
        dset = E(nc.semaphore("dset"))
        d0 = [E(nc.semaphore(f"d0_{i}")) for i in range(2)]
        lnS0 = E(nc.semaphore("lnS0"))
        sch = E(nc.semaphore("sch"))
        a0 = E(nc.semaphore("a0"))
        t0 = E(nc.semaphore("t0"))
        v0 = E(nc.semaphore("v0"))
        tbq = E(nc.semaphore("tbq"))
        vbq = E(nc.semaphore("vbq"))
        scp0 = E(nc.semaphore("scp0"))
        p0 = ExitStack()
        xr_s = [p0.enter_context(nc.sbuf_tensor(f"xr{i}", [128, D], F32)) for i in range(2)]
        xn_s = p0.enter_context(nc.sbuf_tensor("xn_s", [128, D], BF16))
        tr0_s = p0.enter_context(nc.sbuf_tensor("tr0_s", [128, D], BF16))
        tr0v_s = p0.enter_context(nc.sbuf_tensor("tr0v_s", [128, D], BF16))
        bq_stage = p0.enter_context(nc.sbuf_tensor("bq_stage", [NF, 128], F32))
        tp_ps = [p0.enter_context(nc.psum_tensor(f"tp{i}", [128, 1024], BF16)) for i in range(2)]
        bq_ps = p0.enter_context(nc.psum_tensor("bq_ps", [128, NF], F32))
        with p0, nc.Block() as blk:
            @blk.sync
            def _(sync):
                sync.dma_start(id_s[:], id_d).then_inc(dset, 16)
                sync.dma_start(ones_s[:], ones_d).then_inc(dset, 16)
                bq_src = bass.AP(bqkv_d.tensor, 0, [[128, NF], [1, 128]])
                sync.dma_start(bq_stage[:], bq_src).then_inc(dset, 16)
                for t in range(NT):
                    if t >= 2:
                        sync.wait_ge(a0, t - 1)
                    sync.dma_start(xr_s[t % 2][:], x_d[t * 128:(t + 1) * 128, :]).then_inc(d0[t % 2], 16)

            @blk.scalar
            def _(scalar):
                scalar.wait_ge(dset, 48)
                scalar.copy(idf_s[:], id_s[:]).then_inc(scp0, 1)
                ctr = [0]
                for t in range(NT):
                    scalar.wait_ge(d0[t % 2], 16 * (t // 2 + 1))
                    ins = scalar.activation(tr0_s[:], xr_s[t % 2][:], AF.Square,
                                            accum_out=ssq_s[:])
                    ctr[0] += 1; ins.then_inc(sch, 1); scalar.wait_ge(sch, ctr[0])
                    scalar.wait_ge(lnS0, t + 1)
                    ln_chain(scalar, ctr, sch, rstd_s[:, 0:1], nmr_s[:, 0:1])
                    if t >= 1:
                        scalar.wait_ge(t0, 4 * t)
                    ins = scalar.activation(xn_s[:], xr_s[t % 2][:], AF.Identity,
                                            bias=nmr_s[:, 0:1], scale=rstd_s[:, 0:1])
                    ctr[0] += 1; ins.then_inc(sch, 1)
                    scalar.wait_ge(sch, ctr[0])
                    scalar.nop().then_inc(a0, 1)

            @blk.vector
            def _(vector):
                vector.wait_ge(tbq, 1)
                vector.tensor_copy(bqT_s[:], bq_ps[:, :]).then_inc(vbq, 1)
                for t in range(NT):
                    vector.wait_ge(d0[t % 2], 16 * (t // 2 + 1))
                    vector.tensor_scalar(tr0v_s[:], xr_s[t % 2][:], 1.0, None,
                                         ALU.mult, ALU.add, accum_out=sum_s[:]).then_inc(lnS0, 1)
                    for gq in range(4):
                        vector.wait_ge(t0, 4 * t + gq + 1)
                        dst = xnT[:].rearrange("p (k tt) -> p k tt", k=KD)[
                            :, gq * 8:(gq + 1) * 8, t * 128:(t + 1) * 128]
                        vector.tensor_copy(
                            dst, tp_ps[gq % 2][:].rearrange("p (k c) -> p k c", k=8)
                        ).then_inc(v0, 1)

            @blk.tensor
            def _(tensor):
                tensor.wait_ge(scp0, 1)
                tensor.transpose(bq_ps[:, :], bq_stage[:, :], idf_s[0:NF, 0:NF]).then_inc(tbq, 1)
                for t in range(NT):
                    tensor.wait_ge(a0, t + 1)
                    for gq in range(4):
                        G = 4 * t + gq
                        if G >= 2:
                            tensor.wait_ge(v0, G - 1)
                        for k in range(8):
                            kk = gq * 8 + k
                            mm = tensor.transpose(
                                tp_ps[gq % 2][:, k * 128:(k + 1) * 128],
                                xn_s[:, kk * 128:(kk + 1) * 128], id_s[:, :])
                        mm.then_inc(t0, 1)
        barrier(drains=[("vector", v0, 4 * NT), ("vector", vbq, 1),
                        ("scalar", a0, NT), ("tensor", t0, 4 * NT)])
        if upto < -3:
            return nc

        dq1 = [E(nc.semaphore(f"dq1_{i}")) for i in range(2)]
        vc1 = E(nc.semaphore("vc1"))
        tm1 = E(nc.semaphore("tm1"))
        se1 = E(nc.semaphore("se1"))
        do1 = [E(nc.semaphore(f"do1_{i}")) for i in range(2)]
        p1 = ExitStack()
        wqf_s = [p1.enter_context(nc.sbuf_tensor(f"wqf{i}", [128, D], F32)) for i in range(2)]
        wqb_s = [p1.enter_context(nc.sbuf_tensor(f"wqb{i}", [128, D], BF16)) for i in range(2)]
        qv_s = [p1.enter_context(nc.sbuf_tensor(f"qv{i}", [128, T], BF16)) for i in range(2)]
        qk_ps = [p1.enter_context(nc.psum_tensor(f"qk{i}", [128, T], F32)) for i in range(2)]
        with p1, nc.Block() as blk:
            @blk.sync
            def _(sync):
                for j in range(NF):
                    if j >= 2:
                        sync.wait_ge(vc1, j - 1)
                    src = wqkv_d[:, j * 128:(j + 1) * 128].rearrange("(g p) c -> p g c", p=128)
                    dst = wqf_s[j % 2][:].rearrange("p (g c) -> p g c", g=KD)
                    sync.dma_start(dst, src).then_inc(dq1[j % 2], 16)

            @blk.vector
            def _(vector):
                for j in range(NF):
                    vector.wait_ge(dq1[j % 2], 16 * (j // 2 + 1))
                    if j >= 2:
                        vector.wait_ge(tm1, j - 1)
                    vector.tensor_copy(wqb_s[j % 2][:], wqf_s[j % 2][:]).then_inc(vc1, 1)

            @blk.tensor
            def _(tensor):
                for j in range(NF):
                    tensor.wait_ge(vc1, j + 1)
                    if j >= 2:
                        tensor.wait_ge(se1, j - 1)
                    for ch in range(NC):
                        for k in range(KD):
                            mm = tensor.matmul(
                                qk_ps[j % 2][:, ch * 512:(ch + 1) * 512],
                                wqb_s[j % 2][:, k * 128:(k + 1) * 128],
                                xnT[:, k * T + ch * 512: k * T + (ch + 1) * 512],
                                start=(k == 0), stop=(k == KD - 1))
                    mm.then_inc(tm1, 1)

            @blk.scalar
            def _(scalar):
                for j in range(NF):
                    scalar.wait_ge(tm1, j + 1)
                    if j >= 2:
                        scalar.wait_ge(do1[j % 2], 16 * (j // 2))
                    scalar.activation(qv_s[j % 2][:], qk_ps[j % 2][:, :], AF.Identity,
                                      bias=bqT_s[:, j:j + 1]).then_inc(se1, 1)
                    scalar.wait_ge(se1, j + 1)
                    scalar.dma_start(qkvT_d[j], qv_s[j % 2][:]).then_inc(do1[j % 2], 16)
                scalar.wait_ge(do1[0], 16 * ((NF + 1) // 2))
                scalar.wait_ge(do1[1], 16 * (NF // 2))
        p1.close()
        xnT_stk.close()
        barrier(drains=[("tensor", tm1, NF), ("vector", vc1, NF), ("scalar", se1, NF)])
        if upto < -2:
            return nc

        dkt = E(nc.semaphore("dkt"))
        dvt = [E(nc.semaphore(f"dvt{i}")) for i in range(2)]
        dwo = [E(nc.semaphore(f"dwo{i}")) for i in range(2)]
        dqt = [E(nc.semaphore(f"dqt{i}")) for i in range(2)]
        dxr = [E(nc.semaphore(f"dxr{i}")) for i in range(2)]
        tt2 = E(nc.semaphore("tt2"))
        vev2 = E(nc.semaphore("vev2"))
        vwoc = E(nc.semaphore("vwoc"))
        sboc = E(nc.semaphore("sboc"))
        smm = E(nc.semaphore("smm"))
        vmax2 = E(nc.semaphore("vmax2"))
        sexp = E(nc.semaphore("sexp"))
        vrec2 = E(nc.semaphore("vrec2"))
        tat = E(nc.semaphore("tat"))
        vat2 = E(nc.semaphore("vat2"))
        tav = E(nc.semaphore("tav"))
        sao = E(nc.semaphore("sao"))
        tot2 = E(nc.semaphore("tot2"))
        vaoT2 = E(nc.semaphore("vaoT2"))
        tpr = E(nc.semaphore("tpr"))
        pev = E(nc.semaphore("pev"))
        sln2 = E(nc.semaphore("sln2"))
        vx1 = E(nc.semaphore("vx1"))
        dx1w = [E(nc.semaphore(f"dx1w{i}")) for i in range(2)]
        dbo2 = E(nc.semaphore("dbo2"))
        p2 = ExitStack()
        kT_s = p2.enter_context(nc.sbuf_tensor("kT_s", [128, NDK * T], BF16))
        v_s = p2.enter_context(nc.sbuf_tensor("v_s", [128, NKT * DK], BF16))
        wo_s = p2.enter_context(nc.sbuf_tensor("wo_s", [128, NDK * D], BF16))
        qt_s = [p2.enter_context(nc.sbuf_tensor(f"qt{i}", [128, NDK * 128], BF16)) for i in range(2)]
        xr2_s = [p2.enter_context(nc.sbuf_tensor(f"xr2{i}", [128, D], F32)) for i in range(2)]
        a_s = p2.enter_context(nc.sbuf_tensor("a_s", [128, T], BF16))
        at_s = p2.enter_context(nc.sbuf_tensor("at_s", [128, T], BF16))
        ao_s = p2.enter_context(nc.sbuf_tensor("ao_s", [128, DK], BF16))
        aoT_s = p2.enter_context(nc.sbuf_tensor("aoT_s", [128, DK], BF16))
        ap_s = p2.enter_context(nc.sbuf_tensor("ap_s", [128, D], BF16))
        tr2_s = p2.enter_context(nc.sbuf_tensor("tr2_s", [128, D], BF16))
        bo_s = p2.enter_context(nc.sbuf_tensor("bo_s", [1, D], BF16))
        nmx_s = p2.enter_context(nc.sbuf_tensor("nmx_s", [128, 1], F32))
        nmx32_s = p2.enter_context(nc.sbuf_tensor("nmx32_s", [128, 1], F32))
        sume_s = p2.enter_context(nc.sbuf_tensor("sume_s", [128, 1], F32))
        rec_s = p2.enter_context(nc.sbuf_tensor("rec_s", [128, 1], F32))
        sup_s = p2.enter_context(nc.sbuf_tensor("sup_s", [128, 8], F32))
        s_ps = p2.enter_context(nc.psum_tensor("s_ps", [128, max(T, 2048)], F32))
        at_ps = p2.enter_context(nc.psum_tensor("at_ps", [128, max(T, DK)], BF16))
        av_ps = p2.enter_context(nc.psum_tensor("av_ps", [128, DK], F32))
        with p2, nc.Block() as blk:
            @blk.sync
            def _(sync):
                src = bass.AP(qkvT_d.tensor, NDK * 128 * T,
                              [[T, 128], [128 * T, NDK], [1, T]])
                sync.dma_start(kT_s[:].rearrange("p (j tt) -> p j tt", j=NDK), src).then_inc(dkt, 16)
                sync.dma_start(xr2_s[0][0:1, :], bass.AP(bo_d.tensor, 0, [[D, 1], [1, D]])).then_inc(dbo2, 16)
                for j in range(NDK):
                    if j >= 2:
                        sync.wait_ge(tt2, j - 1)
                    vstage = a_s if j % 2 == 0 else at_s
                    sync.dma_start(vstage[:], qkvT_d[2 * NDK + j]).then_inc(dvt[j % 2], 16)
                for j in range(NDK):
                    if j == 0:
                        sync.wait_ge(sboc, 1)
                    if j >= 2:
                        sync.wait_ge(vwoc, j - 1)
                    sync.dma_start(xr2_s[j % 2][:], wo_d[j * 128:(j + 1) * 128, :]).then_inc(dwo[j % 2], 16)
                for t in range(NT):
                    if t >= 2:
                        sync.wait_ge(smm, t - 1)
                        sync.wait_ge(dx1w[t % 2], 16 * (t // 2))
                    if t < 2:
                        sync.wait_ge(vwoc, NDK)
                    src = bass.AP(qkvT_d.tensor, t * 128, [[T, 128], [128 * T, NDK], [1, 128]])
                    sync.dma_start(qt_s[t % 2][:].rearrange("p (j c) -> p j c", j=NDK), src).then_inc(dqt[t % 2], 16)
                    sync.dma_start(xr2_s[t % 2][:], x_d[t * 128:(t + 1) * 128, :]).then_inc(dxr[t % 2], 16)

            @blk.tensor
            def _(tensor):
                for j in range(NDK):
                    tensor.wait_ge(dvt[j % 2], 16 * (j // 2 + 1))
                    if j >= 1:
                        tensor.wait_ge(vev2, j)
                    vstage = a_s if j % 2 == 0 else at_s
                    for kt in range(NKT):
                        mm = tensor.transpose(at_ps[:, kt * 128:(kt + 1) * 128],
                                              vstage[:, kt * 128:(kt + 1) * 128], id_s[:, :])
                    mm.then_inc(tt2, 1)
                for t in range(NT):
                    tensor.wait_ge(dqt[t % 2], 16 * (t // 2 + 1))
                    if t == 0:
                        tensor.wait_ge(dkt, 16)
                        tensor.wait_ge(vev2, NDK)
                    if t >= 1:
                        tensor.wait_ge(sexp, t)
                        tensor.wait_ge(pev, 8 * t)
                    for ch in range(NC):
                        for j in range(NDK):
                            mm = tensor.matmul(
                                s_ps[:, ch * 512:(ch + 1) * 512],
                                qt_s[t % 2][:, j * 128:(j + 1) * 128],
                                kT_s[:, j * T + ch * 512: j * T + (ch + 1) * 512],
                                start=(j == 0), stop=(j == NDK - 1))
                    mm.then_inc(smm, 1)
                    tensor.wait_ge(sexp, t + 1)
                    if t >= 1:
                        tensor.wait_ge(vaoT2, t)
                    for kt in range(NKT):
                        mm = tensor.transpose(at_ps[:, kt * 128:(kt + 1) * 128],
                                              a_s[:, kt * 128:(kt + 1) * 128], id_s[:, :])
                    mm.then_inc(tat, 1)
                    tensor.wait_ge(vat2, t + 1)
                    if t >= 1:
                        tensor.wait_ge(sao, t)
                    for ch in range(2):
                        for kt in range(NKT):
                            mm = tensor.matmul(
                                av_ps[:, ch * 512:(ch + 1) * 512],
                                at_s[:, kt * 128:(kt + 1) * 128],
                                v_s[:, kt * DK + ch * 512: kt * DK + (ch + 1) * 512],
                                start=(kt == 0), stop=(kt == NKT - 1))
                    mm.then_inc(tav, 1)
                    tensor.wait_ge(sao, t + 1)
                    for j in range(NDK):
                        mm = tensor.transpose(at_ps[:, j * 128:(j + 1) * 128],
                                              ao_s[:, j * 128:(j + 1) * 128], id_s[:, :])
                    mm.then_inc(tot2, 1)
                    tensor.wait_ge(vaoT2, t + 1)
                    if t == 0:
                        tensor.wait_ge(vwoc, NDK)
                        tensor.wait_ge(sboc, 1)
                    for c in range(8):
                        if c >= 4:
                            tensor.wait_ge(pev, 8 * t + c - 3)
                        ps = s_ps[:, (c % 4) * 512:((c % 4) + 1) * 512]
                        mm = tensor.matmul(ps, ones_s[:, :], bo_s[:, c * 512:(c + 1) * 512],
                                           start=True, stop=False)
                        for j in range(NDK):
                            mm = tensor.matmul(ps, aoT_s[:, j * 128:(j + 1) * 128],
                                               wo_s[:, j * D + c * 512: j * D + (c + 1) * 512],
                                               start=False, stop=(j == NDK - 1))
                        mm.then_inc(tpr, 1)

            @blk.vector
            def _(vector):
                for j in range(NDK):
                    vector.wait_ge(tt2, j + 1)
                    dst = v_s[:].rearrange("p (kt c) -> p kt c", kt=NKT)[:, :, j * 128:(j + 1) * 128]
                    vector.tensor_copy(dst, at_ps[:, 0:T].rearrange("p (kt c) -> p kt c", kt=NKT)).then_inc(vev2, 1)
                for j in range(NDK):
                    vector.wait_ge(dwo[j % 2], 16 * (j // 2 + 1))
                    vector.tensor_copy(wo_s[:, j * D:(j + 1) * D], xr2_s[j % 2][:]).then_inc(vwoc, 1)
                for t in range(NT):
                    vector.wait_ge(smm, t + 1)
                    if t >= 1:
                        vector.wait_ge(sexp, t)
                    vector.reduce_max(nmx_s[:], s_ps[:, 0:T], axis=AX, negate=True).then_inc(vmax2, 1)
                    vector.wait_ge(sexp, t + 1)
                    vector.reciprocal(rec_s[:], sume_s[:]).then_inc(vrec2, 1)
                    vector.wait_ge(tat, t + 1)
                    vector.tensor_copy(at_s[:], at_ps[:, 0:T]).then_inc(vat2, 1)
                    vector.wait_ge(tot2, t + 1)
                    if t >= 1:
                        vector.wait_ge(tpr, 8 * t)
                    vector.tensor_copy(aoT_s[:], at_ps[:, 0:DK]).then_inc(vaoT2, 1)
                    vector.wait_ge(sln2, t + 1)
                    vector.wait_ge(dxr[t % 2], 16 * (t // 2 + 1))
                    vector.tensor_add(xr2_s[t % 2][:], ap_s[:], xr2_s[t % 2][:]).then_inc(vx1, 1)

            @blk.scalar
            def _(scalar):
                scalar.wait_ge(dbo2, 16)
                scalar.copy(bo_s[:], xr2_s[0][0:1, :]).then_inc(sboc, 1)
                ctr = [9 * NT]
                def step(ins):
                    ctr[0] += 1; ins.then_inc(sch, 1); scalar.wait_ge(sch, ctr[0])
                for t in range(NT):
                    scalar.wait_ge(vmax2, t + 1)
                    if t >= 1:
                        scalar.wait_ge(tat, t)
                        scalar.wait_ge(vrec2, t)
                    step(scalar.mul(nmx32_s[:], nmx_s[:], SCL))
                    scalar.activation(a_s[:], s_ps[:, 0:T], AF.Exp, bias=nmx32_s[:, 0:1],
                                      scale=SCL, accum_out=sume_s[:]).then_inc(sexp, 1)
                    scalar.wait_ge(tav, t + 1)
                    scalar.wait_ge(vrec2, t + 1)
                    scalar.activation(ao_s[:], av_ps[:, :], AF.Copy,
                                      scale=rec_s[:, 0:1]).then_inc(sao, 1)
                    if t >= 1:
                        scalar.wait_ge(vx1, t)
                    for c in range(8):
                        scalar.wait_ge(tpr, 8 * t + c + 1)
                        scalar.activation(ap_s[:, c * 512:(c + 1) * 512],
                                          s_ps[:, (c % 4) * 512:((c % 4) + 1) * 512],
                                          AF.Copy, accum_out=sup_s[:, c:c + 1]).then_inc(pev, 1)
                    scalar.wait_ge(pev, 8 * (t + 1))
                    step(scalar.activation(tr2_s[:, 0:8], sup_s[:], AF.Copy,
                                           accum_out=sum_s[:]))
                    step(scalar.activation(tr2_s[:], ap_s[:], AF.Square,
                                           accum_out=ssq_s[:]))
                    ln_chain(scalar, ctr, sch, rstd_s[:, 0:1], nmr_s[:, 0:1])
                    ins = scalar.activation(ap_s[:], ap_s[:], AF.Identity, bias=nmr_s[:, 0:1],
                                            scale=rstd_s[:, 0:1])
                    ins.then_inc(sln2, 1)
                    scalar.wait_ge(vx1, t + 1)
                    scalar.dma_start(x1_d[t], xr2_s[t % 2][:]).then_inc(dx1w[t % 2], 16)
                scalar.wait_ge(dx1w[0], 16 * ((NT + 1) // 2))
                scalar.wait_ge(dx1w[1], 16 * (NT // 2))
        p2.close()
        barrier(drains=[("tensor", tpr, 8 * NT), ("vector", vx1, NT),
                        ("scalar", sln2, NT)])
        if upto < -1:
            return nc

        xn2T_stk = ExitStack()
        xn2T = xn2T_stk.enter_context(nc.sbuf_tensor("xn2T", [128, KD * T], BF16))
        xg2T_s = xn2T_stk.enter_context(nc.sbuf_tensor("xg2T", [32, T], BF16))
        sxg3 = E(nc.semaphore("sxg3"))
        vxg3 = E(nc.semaphore("vxg3"))
        p3 = ExitStack()
        x1r_s = [p3.enter_context(nc.sbuf_tensor(f"x1r{i}", [128, D], F32)) for i in range(2)]
        xn2_s = p3.enter_context(nc.sbuf_tensor("xn2_s", [128, D], BF16))
        tr3_s = p3.enter_context(nc.sbuf_tensor("tr3_s", [128, D], BF16))
        tr3v_s = p3.enter_context(nc.sbuf_tensor("tr3v_s", [128, D], BF16))
        xg2f_s = p3.enter_context(nc.sbuf_tensor("xg2f", [128, 32], F32))
        xg2b_s = p3.enter_context(nc.sbuf_tensor("xg2b", [128, 32], BF16))
        tp3_ps = [p3.enter_context(nc.psum_tensor(f"tp3{i}", [128, 1024], BF16)) for i in range(2)]
        xg2_ps = p3.enter_context(nc.psum_tensor("xg2_ps", [32, 128], BF16))
        with p3, nc.Block() as blk:
            @blk.sync
            def _(sync):
                for t in range(NT):
                    if t >= 2:
                        sync.wait_ge(a0, NT + t - 1)
                    sync.dma_start(x1r_s[t % 2][:], x1_d[t]).then_inc(dq1[t % 2], 16)

            @blk.scalar
            def _(scalar):
                ctr = [19 * NT]
                for t in range(NT):
                    scalar.wait_ge(dq1[t % 2], 16 * (NF // 2) + 16 * (t // 2 + 1))
                    ins = scalar.activation(tr3_s[:], x1r_s[t % 2][:], AF.Square,
                                            accum_out=ssq_s[:])
                    ctr[0] += 1; ins.then_inc(sch, 1); scalar.wait_ge(sch, ctr[0])
                    scalar.wait_ge(lnS0, NT + t + 1)
                    ln_chain(scalar, ctr, sch, rstd_s[:, 0:1], nmr_s[:, 0:1])
                    if t >= 1:
                        scalar.wait_ge(t0, 4 * NT + 4 * t)
                        scalar.wait_ge(vxg3, t)
                    ins = scalar.activation(xn2_s[:], x1r_s[t % 2][:], AF.Identity,
                                            bias=nmr_s[:, 0:1], scale=rstd_s[:, 0:1])
                    ctr[0] += 1; ins.then_inc(sch, 1)
                    scalar.wait_ge(sch, ctr[0])
                    scalar.nop().then_inc(a0, 1)
                    scalar.wait_ge(vxg3, t + 1)
                    if t >= 1:
                        scalar.wait_ge(tbq, 1 + t)
                    scalar.copy(xg2b_s[:], xg2f_s[:]).then_inc(sxg3, 1)

            @blk.vector
            def _(vector):
                for t in range(NT):
                    vector.wait_ge(dq1[t % 2], 16 * (NF // 2) + 16 * (t // 2 + 1))
                    vector.tensor_scalar(tr3v_s[:], x1r_s[t % 2][:], 1.0, None,
                                         ALU.mult, ALU.add, accum_out=sum_s[:]).then_inc(lnS0, 1)
                    vector.wait_ge(a0, NT + t + 1)
                    vector.reduce_sum(xg2f_s[:], xn2_s[:].rearrange("p (g c) -> p g c", g=32),
                                      axis=AX).then_inc(vxg3, 1)
                    for gq in range(4):
                        vector.wait_ge(t0, 4 * NT + 4 * t + gq + 1)
                        dst = xn2T[:].rearrange("p (k tt) -> p k tt", k=KD)[
                            :, gq * 8:(gq + 1) * 8, t * 128:(t + 1) * 128]
                        vector.tensor_copy(
                            dst, tp3_ps[gq % 2][:].rearrange("p (k c) -> p k c", k=8)
                        ).then_inc(v0, 1)
                    vector.wait_ge(tbq, 1 + t + 1)
                    vector.tensor_copy(xg2T_s[:, t * 128:(t + 1) * 128], xg2_ps[:, :]).then_inc(vbq, 1)

            @blk.tensor
            def _(tensor):
                for t in range(NT):
                    tensor.wait_ge(a0, NT + t + 1)
                    for gq in range(4):
                        G = 4 * t + gq
                        if G >= 2:
                            tensor.wait_ge(v0, 4 * NT + G - 1)
                        for k in range(8):
                            kk = gq * 8 + k
                            mm = tensor.transpose(tp3_ps[gq % 2][:, k * 128:(k + 1) * 128],
                                                  xn2_s[:, kk * 128:(kk + 1) * 128], id_s[:, :])
                        mm.then_inc(t0, 1)
                    tensor.wait_ge(sxg3, t + 1)
                    if t >= 1:
                        tensor.wait_ge(vbq, 1 + t)
                    tensor.transpose(xg2_ps[:, :], xg2b_s[:, :], id_s[:, :]).then_inc(tbq, 1)
        p3.close()
        barrier(drains=[("vector", v0, 8 * NT), ("vector", vbq, 1 + NT),
                        ("tensor", tbq, 1 + NT), ("scalar", sxg3, NT)])
        if upto < 0:
            return nc

        dg4 = [E(nc.semaphore(f"dg4_{i}")) for i in range(2)]
        du4 = [E(nc.semaphore(f"du4_{i}")) for i in range(2)]
        dgn = [E(nc.semaphore(f"dgn{i}")) for i in range(2)]
        vwg = E(nc.semaphore("vwg"))
        vwu = E(nc.semaphore("vwu"))
        tg4 = E(nc.semaphore("tg4"))
        tu4 = E(nc.semaphore("tu4"))
        ssi4 = E(nc.semaphore("ssi4"))
        vh4 = E(nc.semaphore("vh4"))
        dhw = E(nc.semaphore("dhw"))
        p4 = ExitStack()
        sgq_s = [p4.enter_context(nc.sbuf_tensor(f"sgq{i}", [128, 1024], BF16)) for i in range(2)]
        suq_s = [p4.enter_context(nc.sbuf_tensor(f"suq{i}", [128, 1024], BF16)) for i in range(2)]
        qwgq_s = [p4.enter_context(nc.sbuf_tensor(f"qwgq{i}", [128, 1024], I32)) for i in range(2)]
        qwuq_s = [p4.enter_context(nc.sbuf_tensor(f"qwuq{i}", [128, 1024], I32)) for i in range(2)]
        nzg_s = [p4.enter_context(nc.sbuf_tensor(f"nzg{i}", [32, 128], BF16)) for i in range(2)]
        nzu_s = [p4.enter_context(nc.sbuf_tensor(f"nzu{i}", [32, 128], BF16)) for i in range(2)]
        wpg_s = [p4.enter_context(nc.sbuf_tensor(f"wpg{i}", [128, 4096], BF16)) for i in range(2)]
        wpu_s = [p4.enter_context(nc.sbuf_tensor(f"wpu{i}", [128, 4096], BF16)) for i in range(2)]
        sg_sb = p4.enter_context(nc.sbuf_tensor("sg_sb", [128, T], BF16))
        h_s = p4.enter_context(nc.sbuf_tensor("h4", [128, T], BF16))
        gps_g = p4.enter_context(nc.psum_tensor("gps_g", [128, T], F32))
        gps_u = p4.enter_context(nc.psum_tensor("gps_u", [128, T], F32))
        with p4, nc.Block() as blk:
            @blk.sync
            def _(sync):
                for j in range(HT):
                    par = j % 2
                    if j >= 2:
                        sync.wait_ge(tg4, j - 1)
                        sync.wait_ge(tu4, j - 1)
                    sync.dma_start(nzg_s[par][:], nzsg_d[:, j * 128:(j + 1) * 128]).then_inc(dgn[par], 16)
                    sync.dma_start(nzu_s[par][:], nzsu_d[:, j * 128:(j + 1) * 128]).then_inc(dgn[par], 16)
                    for q in range(4):
                        Q = 4 * j + q
                        qpar = Q % 2
                        if Q >= 2:
                            sync.wait_ge(vwg, Q - 1)
                        bsrc = bass.AP(sgf_d.tensor, j * 4096 + q * 1024, [[0, 128], [1, 1024]])
                        sync.dma_start(sgq_s[qpar][:], bsrc).then_inc(dg4[qpar], 16)
                        src = bass.AP(qwg_d.tensor, (q * 8 * 128) * H + j * 128,
                                      [[H, 128], [128 * H, 8], [1, 128]])
                        sync.dma_start(qwgq_s[qpar][:].rearrange("p (g c) -> p g c", g=8), src).then_inc(dg4[qpar], 16)
                        if Q >= 2:
                            sync.wait_ge(vwu, Q - 1)
                        bsrc = bass.AP(suf_d.tensor, j * 4096 + q * 1024, [[0, 128], [1, 1024]])
                        sync.dma_start(suq_s[qpar][:], bsrc).then_inc(du4[qpar], 16)
                        src = bass.AP(qwu_d.tensor, (q * 8 * 128) * H + j * 128,
                                      [[H, 128], [128 * H, 8], [1, 128]])
                        sync.dma_start(qwuq_s[qpar][:].rearrange("p (g c) -> p g c", g=8), src).then_inc(du4[qpar], 16)

            @blk.vector
            def _(vector):
                for j in range(HT):
                    par = j % 2
                    for q in range(4):
                        Q = 4 * j + q
                        qpar = Q % 2
                        vector.wait_ge(dg4[qpar], 32 * (Q // 2 + 1))
                        if j >= 2 and q == 0:
                            vector.wait_ge(tg4, j - 1)
                        vector.scalar_tensor_tensor(
                            wpg_s[par][:, q * 1024:(q + 1) * 1024], qwgq_s[qpar][:], 1.0,
                            sgq_s[qpar][:], ALU.mult, ALU.mult).then_inc(vwg, 1)
                        vector.wait_ge(du4[qpar], 32 * (Q // 2 + 1))
                        if j >= 2 and q == 0:
                            vector.wait_ge(tu4, j - 1)
                        vector.scalar_tensor_tensor(
                            wpu_s[par][:, q * 1024:(q + 1) * 1024], qwuq_s[qpar][:], 1.0,
                            suq_s[qpar][:], ALU.mult, ALU.mult).then_inc(vwu, 1)
                    vector.wait_ge(tu4, j + 1)
                    vector.wait_ge(ssi4, j + 1)
                    if j >= 1:
                        vector.wait_ge(dhw, 16 * j)
                    vector.tensor_tensor(h_s[:], gps_u[:, :], sg_sb[:],
                                         ALU.mult).then_inc(vh4, 1)

            @blk.tensor
            def _(tensor):
                for j in range(HT):
                    par = j % 2
                    tensor.wait_ge(vwg, 4 * (j + 1))
                    tensor.wait_ge(dgn[par], 32 * (j // 2 + 1))
                    if j >= 1:
                        tensor.wait_ge(ssi4, j)
                    for ch in range(NC):
                        ps = gps_g[:, ch * 512:(ch + 1) * 512]
                        mm = tensor.matmul(ps, nzg_s[par][:, :],
                                           xg2T_s[:, ch * 512:(ch + 1) * 512],
                                           start=True, stop=False)
                        for k in range(KD):
                            mm = tensor.matmul(ps, wpg_s[par][:, k * 128:(k + 1) * 128],
                                               xn2T[:, k * T + ch * 512: k * T + (ch + 1) * 512],
                                               start=False, stop=(k == KD - 1))
                    mm.then_inc(tg4, 1)
                    tensor.wait_ge(vwu, 4 * (j + 1))
                    if j >= 1:
                        tensor.wait_ge(vh4, j)
                    for ch in range(NC):
                        ps = gps_u[:, ch * 512:(ch + 1) * 512]
                        mm = tensor.matmul(ps, nzu_s[par][:, :],
                                           xg2T_s[:, ch * 512:(ch + 1) * 512],
                                           start=True, stop=False)
                        for k in range(KD):
                            mm = tensor.matmul(ps, wpu_s[par][:, k * 128:(k + 1) * 128],
                                               xn2T[:, k * T + ch * 512: k * T + (ch + 1) * 512],
                                               start=False, stop=(k == KD - 1))
                    mm.then_inc(tu4, 1)

            @blk.scalar
            def _(scalar):
                for j in range(HT):
                    scalar.wait_ge(tg4, j + 1)
                    if j >= 1:
                        scalar.wait_ge(vh4, j)
                    scalar.activation(sg_sb[:], gps_g[:, :], AF.Silu).then_inc(ssi4, 1)
                    scalar.wait_ge(vh4, j + 1)
                    scalar.dma_start(hT_d[j], h_s[:]).then_inc(dhw, 16)
                scalar.wait_ge(dhw, 16 * HT)
        p4.close()
        xn2T_stk.close()
        barrier(drains=[("tensor", tu4, HT), ("vector", vh4, HT), ("scalar", ssi4, HT)])

        if upto < 5:
            return nc
        NQ = NT // 4
        GB = (HT + 3) // 4
        dqd = [E(nc.semaphore(f"dqd{i}")) for i in range(2)]
        vdq5 = E(nc.semaphore("vdq5"))
        sv5 = E(nc.semaphore("sv5"))
        tmb = E(nc.semaphore("tmb"))
        tm5 = E(nc.semaphore("tm5"))
        se5 = E(nc.semaphore("se5"))
        vq5 = E(nc.semaphore("vq5"))
        dhp = [E(nc.semaphore(f"dhp{i}")) for i in range(4)]
        dfw = [E(nc.semaphore(f"dfw{i}")) for i in range(4)]
        p5 = ExitStack()
        wdc_s = p5.enter_context(nc.sbuf_tensor("wdc", [128, HT * 512], BF16))
        hp_s = [p5.enter_context(nc.sbuf_tensor(f"hp{i}", [128, 4 * 512], BF16)) for i in range(4)]
        st_s = [p5.enter_context(nc.sbuf_tensor(f"st{i}", [128, 512], BF16)) for i in range(4)]
        qwd_st = [p5.enter_context(nc.sbuf_tensor(f"qwdst{i}", [128, 2048], I32)) for i in range(2)]
        sd_st = [p5.enter_context(nc.sbuf_tensor(f"sdst{i}", [128, 2048], BF16)) for i in range(2)]
        nzd_st = [p5.enter_context(nc.sbuf_tensor(f"nzdst{i}", [128, 2048], BF16)) for i in range(2)]
        td_s = p5.enter_context(nc.sbuf_tensor("td5", [128, 2048], BF16))
        tr5_s = p5.enter_context(nc.sbuf_tensor("tr5", [128, 512], BF16))
        dn_ps = [p5.enter_context(nc.psum_tensor(f"dn{i}", [128, 2048], F32)) for i in range(2)]
        with p5, nc.Block() as blk:
            @blk.gpsimd
            def _(gpsimd):
                for c in range(8):
                    for b in range(GB):
                        Bi = c * GB + b
                        par = Bi % 2
                        g0 = b * 4
                        gn = min(4, HT - g0)
                        if Bi >= 2:
                            gpsimd.wait_ge(vdq5, Bi - 1)
                        src = bass.AP(qwd_d.tensor, (g0 * 128) * D + c * 512,
                                      [[D, 128], [128 * D, gn], [1, 512]])
                        gpsimd.dma_start(qwd_st[par][:, 0:gn * 512].rearrange("p (g c2) -> p g c2", g=gn),
                                         src).then_inc(dqd[par], 16)
                        src = bass.AP(sdb_d.tensor, g0 * D + c * 512, [[0, 128], [D, gn], [1, 512]])
                        gpsimd.dma_start(sd_st[par][:, 0:gn * 512].rearrange("p (g c2) -> p g c2", g=gn),
                                         src).then_inc(dqd[par], 16)
                        src = bass.AP(nzd_d.tensor, g0 * D + c * 512, [[0, 128], [D, gn], [1, 512]])
                        gpsimd.dma_start(nzd_st[par][:, 0:gn * 512].rearrange("p (g c2) -> p g c2", g=gn),
                                         src).then_inc(dqd[par], 16)

            @blk.vector
            def _(vector):
                for c in range(9):
                    if c >= 1:
                        cc = c - 1
                        for q in range(NQ):
                            qi = cc * NQ + q
                            for tt in range(4):
                                vector.wait_ge(se5, 4 * qi + tt + 1)
                                if qi * 4 + tt >= 1:
                                    vector.wait_ge(vq5, qi * 4 + tt)
                                sti = (qi * 4 + tt) % 4
                                tg = q * 4 + tt
                                vector.scalar_tensor_tensor(
                                    tr5_s[:], st_s[sti][:], 1.0, st_s[sti][:],
                                    ALU.mult, ALU.mult,
                                    accum_out=ssqp_s[:, tg * 8 + cc:tg * 8 + cc + 1]).then_inc(vq5, 1)
                    if c < 8:
                        for b in range(GB):
                            Bi = c * GB + b
                            par = Bi % 2
                            g0 = b * 4
                            gn = min(4, HT - g0)
                            vector.wait_ge(dqd[par], 48 * (Bi // 2 + 1))
                            if Bi >= 1:
                                vector.wait_ge(vdq5, Bi)
                            if c >= 1:
                                vector.wait_ge(tmb, (c * NQ - 1) * GB + b + 1)
                            vector.scalar_tensor_tensor(
                                td_s[:, 0:gn * 512], qwd_st[par][:, 0:gn * 512], 1.0,
                                sd_st[par][:, 0:gn * 512], ALU.mult, ALU.mult).then_inc(sv5, 1)
                            vector.wait_ge(sv5, Bi + 1)
                            vector.tensor_add(wdc_s[:, g0 * 512:(g0 + gn) * 512],
                                              td_s[:, 0:gn * 512],
                                              nzd_st[par][:, 0:gn * 512]).then_inc(vdq5, 1)

            @blk.tensor
            def _(tensor):
                hseen = [0, 0, 0, 0]
                for c in range(8):
                    for q in range(NQ):
                        qi = c * NQ + q
                        par = qi % 2
                        if qi >= 2:
                            tensor.wait_ge(se5, 4 * (qi - 1))
                        for b in range(GB):
                            g0 = b * 4
                            gn = min(4, HT - g0)
                            bidx = qi * GB + b
                            bi = bidx % 4
                            hseen[bi] += 16
                            tensor.wait_ge(tmb, bidx)
                            tensor.wait_ge(dhp[bi], hseen[bi])
                            tensor.wait_ge(vdq5, c * GB + b + 1)
                            for gg in range(gn):
                                g = g0 + gg
                                for tt in range(4):
                                    mm = tensor.matmul(
                                        dn_ps[par][:, tt * 512:(tt + 1) * 512],
                                        hp_s[bi][:, gg * 512 + tt * 128: gg * 512 + tt * 128 + 128],
                                        wdc_s[:, g * 512:(g + 1) * 512],
                                        start=(g == 0), stop=(g == HT - 1))
                                    if g == HT - 1 and tt < 3:
                                        mm.then_inc(tm5, 1)
                            mm.then_inc(tmb, 1)

            @blk.sync
            def _(sync):
                for c in range(8):
                    for q in range(NQ):
                        qi = c * NQ + q
                        for b in range(GB):
                            g0 = b * 4
                            gn = min(4, HT - g0)
                            bidx = qi * GB + b
                            bi = bidx % 4
                            if bidx >= 4:
                                sync.wait_ge(tmb, bidx - 3)
                            src = bass.AP(hT_d.tensor, (g0 * 128) * T + q * 512,
                                          [[T, 128], [128 * T, gn], [1, 512]])
                            sync.dma_start(hp_s[bi][:, 0:gn * 512].rearrange("p (g c2) -> p g c2", g=gn),
                                           src).then_inc(dhp[bi], 16)

            @blk.scalar
            def _(scalar):
                nst = [0] * 4
                for c in range(8):
                    for q in range(NQ):
                        qi = c * NQ + q
                        par = qi % 2
                        for tt in range(4):
                            gsi = qi * 4 + tt
                            if tt < 3:
                                scalar.wait_ge(tm5, 3 * qi + tt + 1)
                            else:
                                scalar.wait_ge(tmb, (qi + 1) * GB)
                            sti = gsi % 4
                            if nst[sti] > 0:
                                scalar.wait_ge(dfw[sti], 16 * nst[sti])
                                scalar.wait_ge(vq5, gsi - 3)
                            tg = q * 4 + tt
                            scalar.activation(st_s[sti][:], dn_ps[par][:, tt * 512:(tt + 1) * 512],
                                              AF.Copy,
                                              accum_out=sump_s[:, tg * 8 + c:tg * 8 + c + 1]).then_inc(se5, 1)
                            scalar.wait_ge(se5, gsi + 1)
                            scalar.dma_start(
                                bass.AP(fo_d.tensor, (tg * 128) * D + c * 512, [[D, 128], [1, 512]]),
                                st_s[sti][:]).then_inc(dfw[sti], 16)
                            nst[sti] += 1
                ctr = [28 * NT]
                scalar.wait_ge(vq5, 32 * NQ)
                scalar.wait_ge(se5, 32 * NQ)
                for t in range(NT):
                    ins = scalar.activation(tr5_s[:, 0:8], sump_s[:, t * 8:(t + 1) * 8],
                                            AF.Copy, accum_out=sum_s[:])
                    ctr[0] += 1; ins.then_inc(sch, 1); scalar.wait_ge(sch, ctr[0])
                    ins = scalar.activation(tr5_s[:, 0:8], ssqp_s[:, t * 8:(t + 1) * 8],
                                            AF.Copy, accum_out=ssq_s[:])
                    ctr[0] += 1; ins.then_inc(sch, 1); scalar.wait_ge(sch, ctr[0])
                    ln_chain(scalar, ctr, sch, rstd4_s[:, t:t + 1], nmr4_s[:, t:t + 1])
                for i in range(4):
                    scalar.wait_ge(dfw[i], 16 * nst[i])
        p5.close()
        barrier(drains=[("tensor", tmb, 8 * NQ * GB), ("vector", vq5, 32 * NQ),
                        ("scalar", sch, 37 * NT)])

        if upto < 6:
            return nc
        df6 = [E(nc.semaphore(f"df6_{i}")) for i in range(2)]
        dx6 = [E(nc.semaphore(f"dx6_{i}")) for i in range(2)]
        s6 = E(nc.semaphore("s6"))
        v6 = E(nc.semaphore("v6"))
        dy6 = [E(nc.semaphore(f"dy6_{i}")) for i in range(2)]
        p6 = ExitStack()
        fo_s = [p6.enter_context(nc.sbuf_tensor(f"fo6{i}", [128, D], BF16)) for i in range(2)]
        x1f_s = [p6.enter_context(nc.sbuf_tensor(f"x1f{i}", [128, D], F32)) for i in range(2)]
        lnf_s = p6.enter_context(nc.sbuf_tensor("lnf_s", [128, D], F32))
        yo_s = [p6.enter_context(nc.sbuf_tensor(f"yo{i}", [128, D], F32)) for i in range(2)]
        with p6, nc.Block() as blk:
            @blk.sync
            def _(sync):
                for t in range(NT):
                    if t >= 2:
                        sync.wait_ge(v6, t - 1)
                    sync.dma_start(fo_s[t % 2][:], fo_d[t]).then_inc(df6[t % 2], 16)
                    sync.dma_start(x1f_s[t % 2][:], x1_d[t]).then_inc(dx6[t % 2], 16)

            @blk.scalar
            def _(scalar):
                for t in range(NT):
                    scalar.wait_ge(df6[t % 2], 16 * (t // 2 + 1))
                    if t >= 1:
                        scalar.wait_ge(v6, t)
                    scalar.activation(lnf_s[:], fo_s[t % 2][:], AF.Identity,
                                      bias=nmr4_s[:, t:t + 1],
                                      scale=rstd4_s[:, t:t + 1]).then_inc(s6, 1)

            @blk.vector
            def _(vector):
                for t in range(NT):
                    vector.wait_ge(s6, t + 1)
                    vector.wait_ge(dx6[t % 2], 16 * (t // 2 + 1))
                    if t >= 2:
                        vector.wait_ge(dy6[t % 2], 16 * (t // 2))
                    vector.tensor_add(yo_s[t % 2][:], lnf_s[:], x1f_s[t % 2][:]).then_inc(v6, 1)

            @blk.gpsimd
            def _(gpsimd):
                for t in range(NT):
                    gpsimd.wait_ge(v6, t + 1)
                    gpsimd.dma_start(y_d[t * 128:(t + 1) * 128, :], yo_s[t % 2][:]).then_inc(dy6[t % 2], 16)
                gpsimd.wait_ge(dy6[0], 16 * ((NT + 1) // 2))
                gpsimd.wait_ge(dy6[1], 16 * (NT // 2))

    return nc


def make_consts():
    return {
        "ident": np.eye(128, dtype=ml_dtypes.bfloat16),
        "ones": np.ones((1, 128), dtype=ml_dtypes.bfloat16),
    }


def kernel(L0_attn_x, L0_attn_Wqkv, L0_attn_bqkv, L0_attn_Wo, L0_attn_bo,
           L0_ffn_qw_gate, L0_ffn_z_gate, L0_ffn_s_gate,
           L0_ffn_qw_up, L0_ffn_z_up, L0_ffn_s_up,
           L0_ffn_qw_down, L0_ffn_z_down, L0_ffn_s_down):
    x = np.ascontiguousarray(np.asarray(L0_attn_x), dtype=np.float32)
    B, T, D_ = x.shape
    nc = build(T=T, HT=np.asarray(L0_ffn_qw_gate).shape[1] // 128)
    common = {
        "wqkv": np.ascontiguousarray(L0_attn_Wqkv, dtype=np.float32),
        "bqkv": np.ascontiguousarray(L0_attn_bqkv, dtype=np.float32),
        "wo": np.ascontiguousarray(L0_attn_Wo, dtype=np.float32),
        "bo": np.ascontiguousarray(L0_attn_bo, dtype=np.float32),
        "qwg": np.ascontiguousarray(L0_ffn_qw_gate, dtype=np.int32),
        "zg": np.ascontiguousarray(L0_ffn_z_gate, dtype=np.int32),
        "sg": np.ascontiguousarray(L0_ffn_s_gate, dtype=np.float32),
        "qwu": np.ascontiguousarray(L0_ffn_qw_up, dtype=np.int32),
        "zu": np.ascontiguousarray(L0_ffn_z_up, dtype=np.int32),
        "su": np.ascontiguousarray(L0_ffn_s_up, dtype=np.float32),
        "qwd": np.ascontiguousarray(L0_ffn_qw_down, dtype=np.int32),
        "zd": np.ascontiguousarray(L0_ffn_z_down, dtype=np.int32),
        "sd": np.ascontiguousarray(L0_ffn_s_down, dtype=np.float32),
        **make_consts(),
    }
    in_maps = [dict(common, x=np.ascontiguousarray(x[c])) for c in range(B)]
    res = run_bass_kernel_spmd(nc, in_maps, core_ids=list(range(B)))
    return np.stack([np.asarray(r["y"], dtype=np.float32) for r in res.results], axis=0)


# version 04
# speedup vs baseline: 1.0531x; 1.0531x over previous
from contextlib import ExitStack

import numpy as np
import ml_dtypes

import concourse.bass as bass
import concourse.mybir as mybir
from concourse.bass_utils import run_bass_kernel_spmd

F32 = mybir.dt.float32
BF16 = mybir.dt.bfloat16
I32 = mybir.dt.int32
AF = mybir.ActivationFunctionType
ALU = mybir.AluOpType
AX = mybir.AxisListType.X

D = 4096
KD = D // 128
F = 3072
NF = F // 128
DK = 1024
NDK = DK // 128
GI = 32
EPS = 1e-5
SCL = 1.0 / 32.0


def build(T=2048, HT=86, upto=99, dump=None):
    H = 128 * HT
    NT = T // 128
    NC = T // 512
    NKT = T // 128
    nc = bass.Bass("TRN2", target_bir_lowering=False)

    x_d = nc.dram_tensor("x", [T, D], F32, kind="ExternalInput").ap()
    wqkv_d = nc.dram_tensor("wqkv", [D, F], F32, kind="ExternalInput").ap()
    bqkv_d = nc.dram_tensor("bqkv", [F], F32, kind="ExternalInput").ap()
    wo_d = nc.dram_tensor("wo", [DK, D], F32, kind="ExternalInput").ap()
    bo_d = nc.dram_tensor("bo", [D], F32, kind="ExternalInput").ap()
    qwg_d = nc.dram_tensor("qwg", [D, H], I32, kind="ExternalInput").ap()
    zg_d = nc.dram_tensor("zg", [GI, H], I32, kind="ExternalInput").ap()
    sg_d = nc.dram_tensor("sg", [GI, H], F32, kind="ExternalInput").ap()
    qwu_d = nc.dram_tensor("qwu", [D, H], I32, kind="ExternalInput").ap()
    zu_d = nc.dram_tensor("zu", [GI, H], I32, kind="ExternalInput").ap()
    su_d = nc.dram_tensor("su", [GI, H], F32, kind="ExternalInput").ap()
    qwd_d = nc.dram_tensor("qwd", [H, D], I32, kind="ExternalInput").ap()
    zd_d = nc.dram_tensor("zd", [HT, D], I32, kind="ExternalInput").ap()
    sd_d = nc.dram_tensor("sd", [HT, D], F32, kind="ExternalInput").ap()
    id_d = nc.dram_tensor("ident", [128, 128], BF16, kind="ExternalInput").ap()
    ones_d = nc.dram_tensor("ones", [1, 128], BF16, kind="ExternalInput").ap()
    y_d = nc.dram_tensor("y", [T, D], F32, kind="ExternalOutput").ap()

    qkvT_d = nc.dram_tensor("qkvT_v", [NF, 128, T], BF16, kind="Internal").ap()
    x1_d = nc.dram_tensor("x1_v", [NT, 128, D], F32, kind="Internal").ap()
    hT_d = nc.dram_tensor("hT_v", [HT, 128, T], BF16, kind="Internal").ap()
    fo_d = nc.dram_tensor("fo_v", [NT, 128, D], BF16, kind="Internal").ap()
    nzsg_d = nc.dram_tensor("nzsg_v", [GI, H], BF16, kind="Internal").ap()
    nzsu_d = nc.dram_tensor("nzsu_v", [GI, H], BF16, kind="Internal").ap()
    sgb_d = nc.dram_tensor("sgb_v", [GI, H], BF16, kind="Internal").ap()
    sub_d = nc.dram_tensor("sub_v", [GI, H], BF16, kind="Internal").ap()
    sgf_d = nc.dram_tensor("sgf_v", [HT, 4096], BF16, kind="Internal").ap()
    suf_d = nc.dram_tensor("suf_v", [HT, 4096], BF16, kind="Internal").ap()
    sdb_d = nc.dram_tensor("sdb_v", [HT, D], BF16, kind="Internal").ap()
    nzd_d = nc.dram_tensor("nzd_v", [HT, D], BF16, kind="Internal").ap()

    stack = ExitStack()
    E = stack.enter_context
    with stack:
        id_s = E(nc.sbuf_tensor("id_s", [128, 128], BF16))
        idf_s = E(nc.sbuf_tensor("idf_s", [128, 128], F32))
        ones_s = E(nc.sbuf_tensor("ones_s", [1, 128], BF16))
        bqT_s = E(nc.sbuf_tensor("bqT_s", [128, NF], F32))
        sum_s = E(nc.sbuf_tensor("sum_s", [128, 1], F32))
        ssq_s = E(nc.sbuf_tensor("ssq_s", [128, 1], F32))
        nmean_s = E(nc.sbuf_tensor("nmean_s", [128, 1], F32))
        ssqn_s = E(nc.sbuf_tensor("ssqn_s", [128, 1], F32))
        msq_s = E(nc.sbuf_tensor("msq_s", [128, 1], F32))
        var_s = E(nc.sbuf_tensor("var_s", [128, 1], F32))
        lnv_s = E(nc.sbuf_tensor("lnv_s", [128, 1], F32))
        rstd_s = E(nc.sbuf_tensor("rstd_s", [128, 1], F32))
        nmr_s = E(nc.sbuf_tensor("nmr_s", [128, 1], F32))
        sump_s = E(nc.sbuf_tensor("sump_s", [128, NT * 8], F32))
        ssqp_s = E(nc.sbuf_tensor("ssqp_s", [128, NT * 8], F32))
        rstd4_s = E(nc.sbuf_tensor("rstd4_s", [128, NT], F32))
        nmr4_s = E(nc.sbuf_tensor("nmr4_s", [128, NT], F32))

        bar = E(nc.semaphore("bar"))
        NBAR = [0]

        def barrier(drains=()):
            NBAR[0] += 1
            tgt = 5 * NBAR[0]
            dr = {}
            for (en, sem, val) in drains:
                dr.setdefault(en, []).append((sem, val))
            with nc.Block() as blk:
                for engname in ("tensor", "vector", "scalar", "gpsimd", "sync"):
                    def mk(eng, _t=tgt, _en=engname):
                        for (sem, val) in dr.get(_en, []):
                            eng.wait_ge(sem, val)
                        eng.nop().then_inc(bar, 1)
                        eng.wait_ge(bar, _t)
                    getattr(blk, engname)(mk)

        def ln_chain(scalar, ctr, sem, rstd_out, nmr_out):
            def step(ins):
                ctr[0] += 1
                ins.then_inc(sem, 1)
                scalar.wait_ge(sem, ctr[0])
            step(scalar.mul(nmean_s[:], sum_s[:], -1.0 / D))
            step(scalar.mul(ssqn_s[:], ssq_s[:], 1.0 / D))
            step(scalar.activation(msq_s[:], nmean_s[:], AF.Square))
            step(scalar.activation(var_s[:], msq_s[:], AF.Identity, scale=-1.0,
                                   bias=ssqn_s[:, 0:1]))
            step(scalar.activation(lnv_s[:], var_s[:], AF.Ln, bias=np.float32(EPS)))
            step(scalar.activation(rstd_out, lnv_s[:], AF.Exp, scale=-0.5))
            step(scalar.activation(nmr_out, nmean_s[:], AF.Identity, scale=rstd_out))

        jobs = []
        def chunks(total, size):
            out, off = [], 0
            while off < total:
                out.append((off, min(size, total - off)))
                off += size
            return out
        for (z, s, nz, sb, rows, total) in (
                (zg_d, sg_d, nzsg_d, sgb_d, GI, H),
                (zu_d, su_d, nzsu_d, sub_d, GI, H),
                (zd_d, sd_d, nzd_d, sdb_d, HT, D)):
            for (off, w) in chunks(total, 2816):
                jobs.append((z, s, nz, sb, rows, off, w))
        NJOB = len(jobs)
        dpp = [E(nc.semaphore(f"dpp{i}")) for i in range(2)]
        vpp = E(nc.semaphore("vpp"))
        dppo = [E(nc.semaphore(f"dppo{i}")) for i in range(2)]
        dflat = E(nc.semaphore("dflat"))
        ppS = ExitStack()
        ppz = [ppS.enter_context(nc.sbuf_tensor(f"ppz{i}", [128, 2816], I32)) for i in range(2)]
        pps = [ppS.enter_context(nc.sbuf_tensor(f"pps{i}", [128, 2816], F32)) for i in range(2)]
        ppn = [ppS.enter_context(nc.sbuf_tensor(f"ppn{i}", [128, 2816], BF16)) for i in range(2)]
        ppb = [ppS.enter_context(nc.sbuf_tensor(f"ppb{i}", [128, 2816], BF16)) for i in range(2)]
        ppf = ppS.enter_context(nc.sbuf_tensor("ppf", [GI, H], BF16))
        with ppS, nc.Block() as blk:
            @blk.sync
            def _(sync):
                for i, (z, s, nz, sb, rows, off, w) in enumerate(jobs):
                    if i >= 2:
                        sync.wait_ge(vpp, 2 * (i - 1))
                    sync.dma_start(ppz[i % 2][0:rows, 0:w], z[:, off:off + w]).then_inc(dpp[i % 2], 16)
                    sync.dma_start(pps[i % 2][0:rows, 0:w], s[:, off:off + w]).then_inc(dpp[i % 2], 16)

            @blk.vector
            def _(vector):
                for i, (z, s, nz, sb, rows, off, w) in enumerate(jobs):
                    vector.wait_ge(dpp[i % 2], 32 * (i // 2 + 1))
                    if i >= 2:
                        vector.wait_ge(dppo[i % 2], 32 * (i // 2))
                    vector.scalar_tensor_tensor(
                        ppn[i % 2][0:rows, 0:w], ppz[i % 2][0:rows, 0:w], -1.0, pps[i % 2][0:rows, 0:w],
                        ALU.mult, ALU.mult).then_inc(vpp, 1)
                    vector.tensor_copy(ppb[i % 2][0:rows, 0:w], pps[i % 2][0:rows, 0:w]).then_inc(vpp, 1)

            @blk.scalar
            def _(scalar):
                for i, (z, s, nz, sb, rows, off, w) in enumerate(jobs):
                    scalar.wait_ge(vpp, 2 * (i + 1))
                    scalar.dma_start(nz[:, off:off + w], ppn[i % 2][0:rows, 0:w]).then_inc(dppo[i % 2], 16)
                    scalar.dma_start(sb[:, off:off + w], ppb[i % 2][0:rows, 0:w]).then_inc(dppo[i % 2], 16)
                scalar.wait_ge(dppo[0], 32 * ((NJOB + 1) // 2))
                scalar.wait_ge(dppo[1], 32 * (NJOB // 2))
                scalar.nop().then_inc(vpp, 1)

            @blk.gpsimd
            def _(gpsimd):
                gpsimd.wait_ge(vpp, 2 * NJOB + 1)
                nfl = 0
                for (src, dst) in ((sgb_d, sgf_d), (sub_d, suf_d)):
                    gpsimd.dma_start(ppf[0:GI, :], src[:, :]).then_inc(dflat, 16)
                    nfl += 16
                    gpsimd.wait_ge(dflat, nfl)
                    dap = bass.AP(dst.tensor, 0, [[128, GI], [4096, HT], [1, 128]])
                    gpsimd.dma_start(
                        dap, ppf[0:GI, :].rearrange("p (j c) -> p j c", j=HT)
                    ).then_inc(dflat, 16)
                    nfl += 16
                    gpsimd.wait_ge(dflat, nfl)
        barrier()
        if upto < -4:
            return nc

        xnT_stk = ExitStack()
        xnT = xnT_stk.enter_context(nc.sbuf_tensor("xnT", [128, KD * T], BF16))
        dset = E(nc.semaphore("dset"))
        d0 = [E(nc.semaphore(f"d0_{i}")) for i in range(2)]
        lnS0 = E(nc.semaphore("lnS0"))
        sch = E(nc.semaphore("sch"))
        a0 = E(nc.semaphore("a0"))
        t0 = E(nc.semaphore("t0"))
        v0 = E(nc.semaphore("v0"))
        tbq = E(nc.semaphore("tbq"))
        vbq = E(nc.semaphore("vbq"))
        scp0 = E(nc.semaphore("scp0"))
        p0 = ExitStack()
        xr_s = [p0.enter_context(nc.sbuf_tensor(f"xr{i}", [128, D], F32)) for i in range(2)]
        xn_s = p0.enter_context(nc.sbuf_tensor("xn_s", [128, D], BF16))
        tr0_s = p0.enter_context(nc.sbuf_tensor("tr0_s", [128, D], BF16))
        tr0v_s = p0.enter_context(nc.sbuf_tensor("tr0v_s", [128, D], BF16))
        bq_stage = p0.enter_context(nc.sbuf_tensor("bq_stage", [NF, 128], F32))
        tp_ps = [p0.enter_context(nc.psum_tensor(f"tp{i}", [128, 1024], BF16)) for i in range(2)]
        bq_ps = p0.enter_context(nc.psum_tensor("bq_ps", [128, NF], F32))
        with p0, nc.Block() as blk:
            @blk.sync
            def _(sync):
                sync.dma_start(id_s[:], id_d).then_inc(dset, 16)
                sync.dma_start(ones_s[:], ones_d).then_inc(dset, 16)
                bq_src = bass.AP(bqkv_d.tensor, 0, [[128, NF], [1, 128]])
                sync.dma_start(bq_stage[:], bq_src).then_inc(dset, 16)
                for t in range(NT):
                    if t >= 2:
                        sync.wait_ge(a0, t - 1)
                    sync.dma_start(xr_s[t % 2][:], x_d[t * 128:(t + 1) * 128, :]).then_inc(d0[t % 2], 16)

            @blk.scalar
            def _(scalar):
                scalar.wait_ge(dset, 48)
                scalar.copy(idf_s[:], id_s[:]).then_inc(scp0, 1)
                ctr = [0]
                for t in range(NT):
                    scalar.wait_ge(d0[t % 2], 16 * (t // 2 + 1))
                    ins = scalar.activation(tr0_s[:], xr_s[t % 2][:], AF.Square,
                                            accum_out=ssq_s[:])
                    ctr[0] += 1; ins.then_inc(sch, 1); scalar.wait_ge(sch, ctr[0])
                    scalar.wait_ge(lnS0, t + 1)
                    ln_chain(scalar, ctr, sch, rstd_s[:, 0:1], nmr_s[:, 0:1])
                    if t >= 1:
                        scalar.wait_ge(t0, 4 * t)
                    ins = scalar.activation(xn_s[:], xr_s[t % 2][:], AF.Identity,
                                            bias=nmr_s[:, 0:1], scale=rstd_s[:, 0:1])
                    ctr[0] += 1; ins.then_inc(sch, 1)
                    scalar.wait_ge(sch, ctr[0])
                    scalar.nop().then_inc(a0, 1)

            @blk.vector
            def _(vector):
                vector.wait_ge(tbq, 1)
                vector.tensor_copy(bqT_s[:], bq_ps[:, :]).then_inc(vbq, 1)
                for t in range(NT):
                    vector.wait_ge(d0[t % 2], 16 * (t // 2 + 1))
                    vector.tensor_scalar(tr0v_s[:], xr_s[t % 2][:], 1.0, None,
                                         ALU.mult, ALU.add, accum_out=sum_s[:]).then_inc(lnS0, 1)
                    for gq in range(4):
                        vector.wait_ge(t0, 4 * t + gq + 1)
                        dst = xnT[:].rearrange("p (k tt) -> p k tt", k=KD)[
                            :, gq * 8:(gq + 1) * 8, t * 128:(t + 1) * 128]
                        vector.tensor_copy(
                            dst, tp_ps[gq % 2][:].rearrange("p (k c) -> p k c", k=8)
                        ).then_inc(v0, 1)

            @blk.tensor
            def _(tensor):
                tensor.wait_ge(scp0, 1)
                tensor.transpose(bq_ps[:, :], bq_stage[:, :], idf_s[0:NF, 0:NF]).then_inc(tbq, 1)
                for t in range(NT):
                    tensor.wait_ge(a0, t + 1)
                    for gq in range(4):
                        G = 4 * t + gq
                        if G >= 2:
                            tensor.wait_ge(v0, G - 1)
                        for k in range(8):
                            kk = gq * 8 + k
                            mm = tensor.transpose(
                                tp_ps[gq % 2][:, k * 128:(k + 1) * 128],
                                xn_s[:, kk * 128:(kk + 1) * 128], id_s[:, :])
                        mm.then_inc(t0, 1)
        barrier(drains=[("vector", v0, 4 * NT), ("vector", vbq, 1),
                        ("scalar", a0, NT), ("tensor", t0, 4 * NT)])
        if upto < -3:
            return nc

        dq1 = [E(nc.semaphore(f"dq1_{i}")) for i in range(2)]
        vc1 = E(nc.semaphore("vc1"))
        tm1 = E(nc.semaphore("tm1"))
        se1 = E(nc.semaphore("se1"))
        do1 = [E(nc.semaphore(f"do1_{i}")) for i in range(2)]
        p1 = ExitStack()
        wqf_s = [p1.enter_context(nc.sbuf_tensor(f"wqf{i}", [128, D], F32)) for i in range(2)]
        wqb_s = [p1.enter_context(nc.sbuf_tensor(f"wqb{i}", [128, D], BF16)) for i in range(2)]
        qv_s = [p1.enter_context(nc.sbuf_tensor(f"qv{i}", [128, T], BF16)) for i in range(2)]
        qk_ps = [p1.enter_context(nc.psum_tensor(f"qk{i}", [128, T], F32)) for i in range(2)]
        with p1, nc.Block() as blk:
            @blk.sync
            def _(sync):
                for j in range(NF):
                    if j >= 2:
                        sync.wait_ge(vc1, j - 1)
                    src = wqkv_d[:, j * 128:(j + 1) * 128].rearrange("(g p) c -> p g c", p=128)
                    dst = wqf_s[j % 2][:].rearrange("p (g c) -> p g c", g=KD)
                    sync.dma_start(dst, src).then_inc(dq1[j % 2], 16)

            @blk.vector
            def _(vector):
                for j in range(NF):
                    vector.wait_ge(dq1[j % 2], 16 * (j // 2 + 1))
                    if j >= 2:
                        vector.wait_ge(tm1, j - 1)
                    vector.tensor_copy(wqb_s[j % 2][:], wqf_s[j % 2][:]).then_inc(vc1, 1)

            @blk.tensor
            def _(tensor):
                for j in range(NF):
                    tensor.wait_ge(vc1, j + 1)
                    if j >= 2:
                        tensor.wait_ge(se1, j - 1)
                    for ch in range(NC):
                        for k in range(KD):
                            mm = tensor.matmul(
                                qk_ps[j % 2][:, ch * 512:(ch + 1) * 512],
                                wqb_s[j % 2][:, k * 128:(k + 1) * 128],
                                xnT[:, k * T + ch * 512: k * T + (ch + 1) * 512],
                                start=(k == 0), stop=(k == KD - 1))
                    mm.then_inc(tm1, 1)

            @blk.scalar
            def _(scalar):
                for j in range(NF):
                    scalar.wait_ge(tm1, j + 1)
                    if j >= 2:
                        scalar.wait_ge(do1[j % 2], 16 * (j // 2))
                    scalar.activation(qv_s[j % 2][:], qk_ps[j % 2][:, :], AF.Identity,
                                      bias=bqT_s[:, j:j + 1]).then_inc(se1, 1)
                    scalar.wait_ge(se1, j + 1)
                    scalar.dma_start(qkvT_d[j], qv_s[j % 2][:]).then_inc(do1[j % 2], 16)
                scalar.wait_ge(do1[0], 16 * ((NF + 1) // 2))
                scalar.wait_ge(do1[1], 16 * (NF // 2))
        p1.close()
        xnT_stk.close()
        barrier(drains=[("tensor", tm1, NF), ("vector", vc1, NF), ("scalar", se1, NF)])
        if upto < -2:
            return nc

        dkt = E(nc.semaphore("dkt"))
        dvt = [E(nc.semaphore(f"dvt{i}")) for i in range(2)]
        dwo = [E(nc.semaphore(f"dwo{i}")) for i in range(2)]
        dqt = [E(nc.semaphore(f"dqt{i}")) for i in range(2)]
        dxr = [E(nc.semaphore(f"dxr{i}")) for i in range(2)]
        tt2 = E(nc.semaphore("tt2"))
        vev2 = E(nc.semaphore("vev2"))
        vwoc = E(nc.semaphore("vwoc"))
        sboc = E(nc.semaphore("sboc"))
        smm = E(nc.semaphore("smm"))
        vmax2 = E(nc.semaphore("vmax2"))
        sexp = E(nc.semaphore("sexp"))
        vrec2 = E(nc.semaphore("vrec2"))
        tat = E(nc.semaphore("tat"))
        vat2 = E(nc.semaphore("vat2"))
        tav = E(nc.semaphore("tav"))
        sao = E(nc.semaphore("sao"))
        tot2 = E(nc.semaphore("tot2"))
        vaoT2 = E(nc.semaphore("vaoT2"))
        tpr = E(nc.semaphore("tpr"))
        pev = E(nc.semaphore("pev"))
        sln2 = E(nc.semaphore("sln2"))
        vx1 = E(nc.semaphore("vx1"))
        dx1w = [E(nc.semaphore(f"dx1w{i}")) for i in range(2)]
        dbo2 = E(nc.semaphore("dbo2"))
        p2 = ExitStack()
        kT_s = p2.enter_context(nc.sbuf_tensor("kT_s", [128, NDK * T], BF16))
        v_s = p2.enter_context(nc.sbuf_tensor("v_s", [128, NKT * DK], BF16))
        wo_s = p2.enter_context(nc.sbuf_tensor("wo_s", [128, NDK * D], BF16))
        qt_s = [p2.enter_context(nc.sbuf_tensor(f"qt{i}", [128, NDK * 128], BF16)) for i in range(2)]
        xr2_s = [p2.enter_context(nc.sbuf_tensor(f"xr2{i}", [128, D], F32)) for i in range(2)]
        a_s = p2.enter_context(nc.sbuf_tensor("a_s", [128, T], BF16))
        at_s = p2.enter_context(nc.sbuf_tensor("at_s", [128, T], BF16))
        ao_s = p2.enter_context(nc.sbuf_tensor("ao_s", [128, DK], BF16))
        aoT_s = p2.enter_context(nc.sbuf_tensor("aoT_s", [128, DK], BF16))
        ap_s = p2.enter_context(nc.sbuf_tensor("ap_s", [128, D], BF16))
        tr2_s = p2.enter_context(nc.sbuf_tensor("tr2_s", [128, D], BF16))
        bo_s = p2.enter_context(nc.sbuf_tensor("bo_s", [1, D], BF16))
        nmx_s = p2.enter_context(nc.sbuf_tensor("nmx_s", [128, 1], F32))
        nmx32_s = p2.enter_context(nc.sbuf_tensor("nmx32_s", [128, 1], F32))
        sume_s = p2.enter_context(nc.sbuf_tensor("sume_s", [128, 1], F32))
        rec_s = p2.enter_context(nc.sbuf_tensor("rec_s", [128, 1], F32))
        sup_s = p2.enter_context(nc.sbuf_tensor("sup_s", [128, 8], F32))
        s_ps = p2.enter_context(nc.psum_tensor("s_ps", [128, max(T, 2048)], F32))
        at_ps = p2.enter_context(nc.psum_tensor("at_ps", [128, max(T, DK)], BF16))
        av_ps = p2.enter_context(nc.psum_tensor("av_ps", [128, DK], F32))
        with p2, nc.Block() as blk:
            @blk.sync
            def _(sync):
                src = bass.AP(qkvT_d.tensor, NDK * 128 * T,
                              [[T, 128], [128 * T, NDK], [1, T]])
                sync.dma_start(kT_s[:].rearrange("p (j tt) -> p j tt", j=NDK), src).then_inc(dkt, 16)
                sync.dma_start(xr2_s[0][0:1, :], bass.AP(bo_d.tensor, 0, [[D, 1], [1, D]])).then_inc(dbo2, 16)
                for j in range(NDK):
                    if j >= 2:
                        sync.wait_ge(tt2, j - 1)
                    vstage = a_s if j % 2 == 0 else at_s
                    sync.dma_start(vstage[:], qkvT_d[2 * NDK + j]).then_inc(dvt[j % 2], 16)
                for j in range(NDK):
                    if j == 0:
                        sync.wait_ge(sboc, 1)
                    if j >= 2:
                        sync.wait_ge(vwoc, j - 1)
                    sync.dma_start(xr2_s[j % 2][:], wo_d[j * 128:(j + 1) * 128, :]).then_inc(dwo[j % 2], 16)
                for t in range(NT):
                    if t >= 2:
                        sync.wait_ge(smm, t - 1)
                        sync.wait_ge(dx1w[t % 2], 16 * (t // 2))
                    if t < 2:
                        sync.wait_ge(vwoc, NDK)
                    src = bass.AP(qkvT_d.tensor, t * 128, [[T, 128], [128 * T, NDK], [1, 128]])
                    sync.dma_start(qt_s[t % 2][:].rearrange("p (j c) -> p j c", j=NDK), src).then_inc(dqt[t % 2], 16)
                    sync.dma_start(xr2_s[t % 2][:], x_d[t * 128:(t + 1) * 128, :]).then_inc(dxr[t % 2], 16)

            @blk.tensor
            def _(tensor):
                for j in range(NDK):
                    tensor.wait_ge(dvt[j % 2], 16 * (j // 2 + 1))
                    if j >= 1:
                        tensor.wait_ge(vev2, j)
                    vstage = a_s if j % 2 == 0 else at_s
                    for kt in range(NKT):
                        mm = tensor.transpose(at_ps[:, kt * 128:(kt + 1) * 128],
                                              vstage[:, kt * 128:(kt + 1) * 128], id_s[:, :])
                    mm.then_inc(tt2, 1)
                for t in range(NT):
                    tensor.wait_ge(dqt[t % 2], 16 * (t // 2 + 1))
                    if t == 0:
                        tensor.wait_ge(dkt, 16)
                        tensor.wait_ge(vev2, NDK)
                    if t >= 1:
                        tensor.wait_ge(sexp, t)
                        tensor.wait_ge(pev, 8 * t)
                    for ch in range(NC):
                        for j in range(NDK):
                            mm = tensor.matmul(
                                s_ps[:, ch * 512:(ch + 1) * 512],
                                qt_s[t % 2][:, j * 128:(j + 1) * 128],
                                kT_s[:, j * T + ch * 512: j * T + (ch + 1) * 512],
                                start=(j == 0), stop=(j == NDK - 1))
                    mm.then_inc(smm, 1)
                    tensor.wait_ge(sexp, t + 1)
                    if t >= 1:
                        tensor.wait_ge(vaoT2, t)
                    for kt in range(NKT):
                        mm = tensor.transpose(at_ps[:, kt * 128:(kt + 1) * 128],
                                              a_s[:, kt * 128:(kt + 1) * 128], id_s[:, :])
                    mm.then_inc(tat, 1)
                    tensor.wait_ge(vat2, t + 1)
                    if t >= 1:
                        tensor.wait_ge(sao, t)
                    for ch in range(2):
                        for kt in range(NKT):
                            mm = tensor.matmul(
                                av_ps[:, ch * 512:(ch + 1) * 512],
                                at_s[:, kt * 128:(kt + 1) * 128],
                                v_s[:, kt * DK + ch * 512: kt * DK + (ch + 1) * 512],
                                start=(kt == 0), stop=(kt == NKT - 1))
                    mm.then_inc(tav, 1)
                    tensor.wait_ge(sao, t + 1)
                    for j in range(NDK):
                        mm = tensor.transpose(at_ps[:, j * 128:(j + 1) * 128],
                                              ao_s[:, j * 128:(j + 1) * 128], id_s[:, :])
                    mm.then_inc(tot2, 1)
                    tensor.wait_ge(vaoT2, t + 1)
                    if t == 0:
                        tensor.wait_ge(vwoc, NDK)
                        tensor.wait_ge(sboc, 1)
                    for c in range(8):
                        if c >= 4:
                            tensor.wait_ge(pev, 8 * t + c - 3)
                        ps = s_ps[:, (c % 4) * 512:((c % 4) + 1) * 512]
                        mm = tensor.matmul(ps, ones_s[:, :], bo_s[:, c * 512:(c + 1) * 512],
                                           start=True, stop=False)
                        for j in range(NDK):
                            mm = tensor.matmul(ps, aoT_s[:, j * 128:(j + 1) * 128],
                                               wo_s[:, j * D + c * 512: j * D + (c + 1) * 512],
                                               start=False, stop=(j == NDK - 1))
                        mm.then_inc(tpr, 1)

            @blk.vector
            def _(vector):
                for j in range(NDK):
                    vector.wait_ge(tt2, j + 1)
                    dst = v_s[:].rearrange("p (kt c) -> p kt c", kt=NKT)[:, :, j * 128:(j + 1) * 128]
                    vector.tensor_copy(dst, at_ps[:, 0:T].rearrange("p (kt c) -> p kt c", kt=NKT)).then_inc(vev2, 1)
                for j in range(NDK):
                    vector.wait_ge(dwo[j % 2], 16 * (j // 2 + 1))
                    vector.tensor_copy(wo_s[:, j * D:(j + 1) * D], xr2_s[j % 2][:]).then_inc(vwoc, 1)
                for t in range(NT):
                    vector.wait_ge(smm, t + 1)
                    if t >= 1:
                        vector.wait_ge(sexp, t)
                    vector.reduce_max(nmx_s[:], s_ps[:, 0:T], axis=AX, negate=True).then_inc(vmax2, 1)
                    vector.wait_ge(sexp, t + 1)
                    vector.reciprocal(rec_s[:], sume_s[:]).then_inc(vrec2, 1)
                    vector.wait_ge(tat, t + 1)
                    vector.tensor_copy(at_s[:], at_ps[:, 0:T]).then_inc(vat2, 1)
                    vector.wait_ge(tot2, t + 1)
                    if t >= 1:
                        vector.wait_ge(tpr, 8 * t)
                    vector.tensor_copy(aoT_s[:], at_ps[:, 0:DK]).then_inc(vaoT2, 1)
                    vector.wait_ge(sln2, t + 1)
                    vector.wait_ge(dxr[t % 2], 16 * (t // 2 + 1))
                    vector.tensor_add(xr2_s[t % 2][:], ap_s[:], xr2_s[t % 2][:]).then_inc(vx1, 1)

            @blk.scalar
            def _(scalar):
                scalar.wait_ge(dbo2, 16)
                scalar.copy(bo_s[:], xr2_s[0][0:1, :]).then_inc(sboc, 1)
                ctr = [9 * NT]
                def step(ins):
                    ctr[0] += 1; ins.then_inc(sch, 1); scalar.wait_ge(sch, ctr[0])
                for t in range(NT):
                    scalar.wait_ge(vmax2, t + 1)
                    if t >= 1:
                        scalar.wait_ge(tat, t)
                        scalar.wait_ge(vrec2, t)
                    step(scalar.mul(nmx32_s[:], nmx_s[:], SCL))
                    scalar.activation(a_s[:], s_ps[:, 0:T], AF.Exp, bias=nmx32_s[:, 0:1],
                                      scale=SCL, accum_out=sume_s[:]).then_inc(sexp, 1)
                    scalar.wait_ge(tav, t + 1)
                    scalar.wait_ge(vrec2, t + 1)
                    scalar.activation(ao_s[:], av_ps[:, :], AF.Copy,
                                      scale=rec_s[:, 0:1]).then_inc(sao, 1)
                    if t >= 1:
                        scalar.wait_ge(vx1, t)
                    for c in range(8):
                        scalar.wait_ge(tpr, 8 * t + c + 1)
                        scalar.activation(ap_s[:, c * 512:(c + 1) * 512],
                                          s_ps[:, (c % 4) * 512:((c % 4) + 1) * 512],
                                          AF.Copy, accum_out=sup_s[:, c:c + 1]).then_inc(pev, 1)
                    scalar.wait_ge(pev, 8 * (t + 1))
                    step(scalar.activation(tr2_s[:, 0:8], sup_s[:], AF.Copy,
                                           accum_out=sum_s[:]))
                    step(scalar.activation(tr2_s[:], ap_s[:], AF.Square,
                                           accum_out=ssq_s[:]))
                    ln_chain(scalar, ctr, sch, rstd_s[:, 0:1], nmr_s[:, 0:1])
                    ins = scalar.activation(ap_s[:], ap_s[:], AF.Identity, bias=nmr_s[:, 0:1],
                                            scale=rstd_s[:, 0:1])
                    ins.then_inc(sln2, 1)
                    scalar.wait_ge(vx1, t + 1)
                    scalar.dma_start(x1_d[t], xr2_s[t % 2][:]).then_inc(dx1w[t % 2], 16)
                scalar.wait_ge(dx1w[0], 16 * ((NT + 1) // 2))
                scalar.wait_ge(dx1w[1], 16 * (NT // 2))
        p2.close()
        barrier(drains=[("tensor", tpr, 8 * NT), ("vector", vx1, NT),
                        ("scalar", sln2, NT)])
        if upto < -1:
            return nc

        xn2T_stk = ExitStack()
        xn2T = xn2T_stk.enter_context(nc.sbuf_tensor("xn2T", [128, KD * T], BF16))
        xg2T_s = xn2T_stk.enter_context(nc.sbuf_tensor("xg2T", [32, T], BF16))
        sxg3 = E(nc.semaphore("sxg3"))
        vxg3 = E(nc.semaphore("vxg3"))
        p3 = ExitStack()
        x1r_s = [p3.enter_context(nc.sbuf_tensor(f"x1r{i}", [128, D], F32)) for i in range(2)]
        xn2_s = p3.enter_context(nc.sbuf_tensor("xn2_s", [128, D], BF16))
        tr3_s = p3.enter_context(nc.sbuf_tensor("tr3_s", [128, D], BF16))
        tr3v_s = p3.enter_context(nc.sbuf_tensor("tr3v_s", [128, D], BF16))
        xg2f_s = p3.enter_context(nc.sbuf_tensor("xg2f", [128, 32], F32))
        xg2b_s = p3.enter_context(nc.sbuf_tensor("xg2b", [128, 32], BF16))
        tp3_ps = [p3.enter_context(nc.psum_tensor(f"tp3{i}", [128, 1024], BF16)) for i in range(2)]
        xg2_ps = p3.enter_context(nc.psum_tensor("xg2_ps", [32, 128], BF16))
        with p3, nc.Block() as blk:
            @blk.sync
            def _(sync):
                for t in range(NT):
                    if t >= 2:
                        sync.wait_ge(a0, NT + t - 1)
                    sync.dma_start(x1r_s[t % 2][:], x1_d[t]).then_inc(dq1[t % 2], 16)

            @blk.scalar
            def _(scalar):
                ctr = [19 * NT]
                for t in range(NT):
                    scalar.wait_ge(dq1[t % 2], 16 * (NF // 2) + 16 * (t // 2 + 1))
                    ins = scalar.activation(tr3_s[:], x1r_s[t % 2][:], AF.Square,
                                            accum_out=ssq_s[:])
                    ctr[0] += 1; ins.then_inc(sch, 1); scalar.wait_ge(sch, ctr[0])
                    scalar.wait_ge(lnS0, NT + t + 1)
                    ln_chain(scalar, ctr, sch, rstd_s[:, 0:1], nmr_s[:, 0:1])
                    if t >= 1:
                        scalar.wait_ge(t0, 4 * NT + 4 * t)
                        scalar.wait_ge(vxg3, t)
                    ins = scalar.activation(xn2_s[:], x1r_s[t % 2][:], AF.Identity,
                                            bias=nmr_s[:, 0:1], scale=rstd_s[:, 0:1])
                    ctr[0] += 1; ins.then_inc(sch, 1)
                    scalar.wait_ge(sch, ctr[0])
                    scalar.nop().then_inc(a0, 1)
                    scalar.wait_ge(vxg3, t + 1)
                    if t >= 1:
                        scalar.wait_ge(tbq, 1 + t)
                    scalar.copy(xg2b_s[:], xg2f_s[:]).then_inc(sxg3, 1)

            @blk.vector
            def _(vector):
                for t in range(NT):
                    vector.wait_ge(dq1[t % 2], 16 * (NF // 2) + 16 * (t // 2 + 1))
                    vector.tensor_scalar(tr3v_s[:], x1r_s[t % 2][:], 1.0, None,
                                         ALU.mult, ALU.add, accum_out=sum_s[:]).then_inc(lnS0, 1)
                    vector.wait_ge(a0, NT + t + 1)
                    vector.reduce_sum(xg2f_s[:], xn2_s[:].rearrange("p (g c) -> p g c", g=32),
                                      axis=AX).then_inc(vxg3, 1)
                    for gq in range(4):
                        vector.wait_ge(t0, 4 * NT + 4 * t + gq + 1)
                        dst = xn2T[:].rearrange("p (k tt) -> p k tt", k=KD)[
                            :, gq * 8:(gq + 1) * 8, t * 128:(t + 1) * 128]
                        vector.tensor_copy(
                            dst, tp3_ps[gq % 2][:].rearrange("p (k c) -> p k c", k=8)
                        ).then_inc(v0, 1)
                    vector.wait_ge(tbq, 1 + t + 1)
                    vector.tensor_copy(xg2T_s[:, t * 128:(t + 1) * 128], xg2_ps[:, :]).then_inc(vbq, 1)

            @blk.tensor
            def _(tensor):
                for t in range(NT):
                    tensor.wait_ge(a0, NT + t + 1)
                    for gq in range(4):
                        G = 4 * t + gq
                        if G >= 2:
                            tensor.wait_ge(v0, 4 * NT + G - 1)
                        for k in range(8):
                            kk = gq * 8 + k
                            mm = tensor.transpose(tp3_ps[gq % 2][:, k * 128:(k + 1) * 128],
                                                  xn2_s[:, kk * 128:(kk + 1) * 128], id_s[:, :])
                        mm.then_inc(t0, 1)
                    tensor.wait_ge(sxg3, t + 1)
                    if t >= 1:
                        tensor.wait_ge(vbq, 1 + t)
                    tensor.transpose(xg2_ps[:, :], xg2b_s[:, :], id_s[:, :]).then_inc(tbq, 1)
        p3.close()
        barrier(drains=[("vector", v0, 8 * NT), ("vector", vbq, 1 + NT),
                        ("tensor", tbq, 1 + NT), ("scalar", sxg3, NT)])
        if upto < 0:
            return nc

        QWR = 3
        SR = 3
        WR = 6
        dqg = [E(nc.semaphore(f"dqg{i}")) for i in range(QWR)]
        dqu = [E(nc.semaphore(f"dqu{i}")) for i in range(QWR)]
        dgn = [E(nc.semaphore(f"dgn{i}")) for i in range(2)]
        dun = [E(nc.semaphore(f"dun{i}")) for i in range(2)]
        vwg = E(nc.semaphore("vwg"))
        vwu = E(nc.semaphore("vwu"))
        tgq = E(nc.semaphore("tgq"))
        tuq = E(nc.semaphore("tuq"))
        ssi4 = E(nc.semaphore("ssi4"))
        vh4 = E(nc.semaphore("vh4"))
        dhw = E(nc.semaphore("dhw"))
        p4 = ExitStack()
        sgq_s = [p4.enter_context(nc.sbuf_tensor(f"sgq{i}", [128, 1024], BF16)) for i in range(SR)]
        suq_s = [p4.enter_context(nc.sbuf_tensor(f"suq{i}", [128, 1024], BF16)) for i in range(SR)]
        qwgq_s = [p4.enter_context(nc.sbuf_tensor(f"qwgq{i}", [128, 1024], I32)) for i in range(QWR)]
        qwuq_s = [p4.enter_context(nc.sbuf_tensor(f"qwuq{i}", [128, 1024], I32)) for i in range(QWR)]
        nzg_s = [p4.enter_context(nc.sbuf_tensor(f"nzg{i}", [32, 128], BF16)) for i in range(2)]
        nzu_s = [p4.enter_context(nc.sbuf_tensor(f"nzu{i}", [32, 128], BF16)) for i in range(2)]
        wpg_s = [p4.enter_context(nc.sbuf_tensor(f"wpg{i}", [128, 1024], BF16)) for i in range(WR)]
        wpu_s = [p4.enter_context(nc.sbuf_tensor(f"wpu{i}", [128, 1024], BF16)) for i in range(WR)]
        sg_sb = p4.enter_context(nc.sbuf_tensor("sg_sb", [128, T], BF16))
        h_s = p4.enter_context(nc.sbuf_tensor("h4", [128, T], BF16))
        gps_g = p4.enter_context(nc.psum_tensor("gps_g", [128, T], F32))
        gps_u = p4.enter_context(nc.psum_tensor("gps_u", [128, T], F32))
        with p4, nc.Block() as blk:
            @blk.sync
            def _(sync):
                for j in range(HT):
                    par = j % 2
                    if j >= 2:
                        sync.wait_ge(tgq, 4 * (j - 1))
                    sync.dma_start(nzg_s[par][:], nzsg_d[:, j * 128:(j + 1) * 128]).then_inc(dgn[par], 16)
                    for q in range(4):
                        Q = 4 * j + q
                        if Q >= QWR:
                            sync.wait_ge(vwg, Q - QWR + 1)
                        src = bass.AP(qwg_d.tensor, (q * 8 * 128) * H + j * 128,
                                      [[H, 128], [128 * H, 8], [1, 128]])
                        sync.dma_start(qwgq_s[Q % QWR][:].rearrange("p (g c) -> p g c", g=8),
                                       src).then_inc(dqg[Q % QWR], 16)
                        bsrc = bass.AP(sgf_d.tensor, j * 4096 + q * 1024, [[0, 128], [1, 1024]])
                        sync.dma_start(sgq_s[Q % SR][:], bsrc).then_inc(dqg[Q % QWR], 16)

            @blk.gpsimd
            def _(gpsimd):
                for j in range(HT):
                    par = j % 2
                    if j >= 2:
                        gpsimd.wait_ge(tuq, 4 * (j - 1))
                    gpsimd.dma_start(nzu_s[par][:], nzsu_d[:, j * 128:(j + 1) * 128]).then_inc(dun[par], 16)
                    for q in range(4):
                        Q = 4 * j + q
                        if Q >= QWR:
                            gpsimd.wait_ge(vwu, Q - QWR + 1)
                        src = bass.AP(qwu_d.tensor, (q * 8 * 128) * H + j * 128,
                                      [[H, 128], [128 * H, 8], [1, 128]])
                        gpsimd.dma_start(qwuq_s[Q % QWR][:].rearrange("p (g c) -> p g c", g=8),
                                         src).then_inc(dqu[Q % QWR], 16)
                        bsrc = bass.AP(suf_d.tensor, j * 4096 + q * 1024, [[0, 128], [1, 1024]])
                        gpsimd.dma_start(suq_s[Q % SR][:], bsrc).then_inc(dqu[Q % QWR], 16)

            @blk.vector
            def _(vector):
                for j in range(HT):
                    for q in range(4):
                        Q = 4 * j + q
                        vector.wait_ge(dqg[Q % QWR], 32 * (Q // QWR + 1))
                        if Q >= WR:
                            vector.wait_ge(tgq, Q - WR + 1)
                        vector.scalar_tensor_tensor(
                            wpg_s[Q % WR][:], qwgq_s[Q % QWR][:], 1.0,
                            sgq_s[Q % SR][:], ALU.mult, ALU.mult).then_inc(vwg, 1)
                    for q in range(4):
                        Q = 4 * j + q
                        vector.wait_ge(dqu[Q % QWR], 32 * (Q // QWR + 1))
                        if Q >= WR:
                            vector.wait_ge(tuq, Q - WR + 1)
                        vector.scalar_tensor_tensor(
                            wpu_s[Q % WR][:], qwuq_s[Q % QWR][:], 1.0,
                            suq_s[Q % SR][:], ALU.mult, ALU.mult).then_inc(vwu, 1)
                    vector.wait_ge(tuq, 4 * (j + 1))
                    vector.wait_ge(ssi4, j + 1)
                    if j >= 1:
                        vector.wait_ge(dhw, 16 * j)
                    vector.tensor_tensor(h_s[:], gps_u[:, :], sg_sb[:],
                                         ALU.mult).then_inc(vh4, 1)

            @blk.tensor
            def _(tensor):
                for j in range(HT):
                    par = j % 2
                    tensor.wait_ge(dgn[par], 16 * (j // 2 + 1))
                    if j >= 1:
                        tensor.wait_ge(ssi4, j)
                    for ch in range(NC):
                        tensor.matmul(gps_g[:, ch * 512:(ch + 1) * 512], nzg_s[par][:, :],
                                      xg2T_s[:, ch * 512:(ch + 1) * 512],
                                      start=True, stop=False)
                    for k in range(KD):
                        Q = 4 * j + k // 8
                        if k % 8 == 0:
                            tensor.wait_ge(vwg, Q + 1)
                        w = wpg_s[Q % WR][:, (k % 8) * 128:(k % 8 + 1) * 128]
                        for ch in range(NC):
                            mm = tensor.matmul(gps_g[:, ch * 512:(ch + 1) * 512], w,
                                               xn2T[:, k * T + ch * 512: k * T + (ch + 1) * 512],
                                               start=False, stop=(k == KD - 1))
                        if k % 8 == 7:
                            mm.then_inc(tgq, 1)
                    tensor.wait_ge(dun[par], 16 * (j // 2 + 1))
                    if j >= 1:
                        tensor.wait_ge(vh4, j)
                    for ch in range(NC):
                        tensor.matmul(gps_u[:, ch * 512:(ch + 1) * 512], nzu_s[par][:, :],
                                      xg2T_s[:, ch * 512:(ch + 1) * 512],
                                      start=True, stop=False)
                    for k in range(KD):
                        Q = 4 * j + k // 8
                        if k % 8 == 0:
                            tensor.wait_ge(vwu, Q + 1)
                        w = wpu_s[Q % WR][:, (k % 8) * 128:(k % 8 + 1) * 128]
                        for ch in range(NC):
                            mm = tensor.matmul(gps_u[:, ch * 512:(ch + 1) * 512], w,
                                               xn2T[:, k * T + ch * 512: k * T + (ch + 1) * 512],
                                               start=False, stop=(k == KD - 1))
                        if k % 8 == 7:
                            mm.then_inc(tuq, 1)

            @blk.scalar
            def _(scalar):
                for j in range(HT):
                    scalar.wait_ge(tgq, 4 * (j + 1))
                    if j >= 1:
                        scalar.wait_ge(vh4, j)
                    scalar.activation(sg_sb[:], gps_g[:, :], AF.Silu).then_inc(ssi4, 1)
                    scalar.wait_ge(vh4, j + 1)
                    scalar.dma_start(hT_d[j], h_s[:]).then_inc(dhw, 16)
                scalar.wait_ge(dhw, 16 * HT)
        p4.close()
        xn2T_stk.close()
        barrier(drains=[("tensor", tuq, 4 * HT), ("vector", vh4, HT), ("scalar", ssi4, HT)])

        if upto < 5:
            return nc
        NQ = NT // 4
        GB = (HT + 3) // 4
        DQR = 3
        HPR = 8
        dqd = [E(nc.semaphore(f"dqd{i}")) for i in range(DQR)]
        vdq5 = E(nc.semaphore("vdq5"))
        sv5 = E(nc.semaphore("sv5"))
        tmb = E(nc.semaphore("tmb"))
        tm5 = E(nc.semaphore("tm5"))
        se5 = E(nc.semaphore("se5"))
        dhp = [E(nc.semaphore(f"dhp{i}")) for i in range(4)]
        dfw = [E(nc.semaphore(f"dfw{i}")) for i in range(4)]
        p5 = ExitStack()
        wdc_s = p5.enter_context(nc.sbuf_tensor("wdc", [128, HT * 512], BF16))
        hp_s = [p5.enter_context(nc.sbuf_tensor(f"hp{i}", [128, 4 * 512], BF16)) for i in range(HPR)]
        st_s = [p5.enter_context(nc.sbuf_tensor(f"st{i}", [128, 512], BF16)) for i in range(4)]
        qwd_st = [p5.enter_context(nc.sbuf_tensor(f"qwdst{i}", [128, 2048], I32)) for i in range(DQR)]
        sd_st = [p5.enter_context(nc.sbuf_tensor(f"sdst{i}", [128, 2048], BF16)) for i in range(DQR)]
        nzd_st = [p5.enter_context(nc.sbuf_tensor(f"nzdst{i}", [128, 2048], BF16)) for i in range(DQR)]
        td_s = p5.enter_context(nc.sbuf_tensor("td5", [128, 2048], BF16))
        sq5_s = p5.enter_context(nc.sbuf_tensor("sq5", [128, 512], BF16))
        dn_ps = [p5.enter_context(nc.psum_tensor(f"dn{i}", [128, 2048], F32)) for i in range(2)]
        with p5, nc.Block() as blk:
            @blk.gpsimd
            def _(gpsimd):
                for c in range(8):
                    for b in range(GB):
                        Bi = c * GB + b
                        par = Bi % DQR
                        g0 = b * 4
                        gn = min(4, HT - g0)
                        if Bi >= DQR:
                            gpsimd.wait_ge(vdq5, Bi - DQR + 1)
                        src = bass.AP(qwd_d.tensor, (g0 * 128) * D + c * 512,
                                      [[D, 128], [128 * D, gn], [1, 512]])
                        gpsimd.dma_start(qwd_st[par][:, 0:gn * 512].rearrange("p (g c2) -> p g c2", g=gn),
                                         src).then_inc(dqd[par], 16)
                        src = bass.AP(sdb_d.tensor, g0 * D + c * 512, [[0, 128], [D, gn], [1, 512]])
                        gpsimd.dma_start(sd_st[par][:, 0:gn * 512].rearrange("p (g c2) -> p g c2", g=gn),
                                         src).then_inc(dqd[par], 16)
                        src = bass.AP(nzd_d.tensor, g0 * D + c * 512, [[0, 128], [D, gn], [1, 512]])
                        gpsimd.dma_start(nzd_st[par][:, 0:gn * 512].rearrange("p (g c2) -> p g c2", g=gn),
                                         src).then_inc(dqd[par], 16)

            @blk.vector
            def _(vector):
                for c in range(8):
                    for b in range(GB):
                        Bi = c * GB + b
                        par = Bi % DQR
                        g0 = b * 4
                        gn = min(4, HT - g0)
                        vector.wait_ge(dqd[par], 48 * (Bi // DQR + 1))
                        if Bi >= 1:
                            vector.wait_ge(vdq5, Bi)
                        if c >= 1:
                            vector.wait_ge(tmb, (c * NQ - 1) * GB + b + 1)
                        vector.scalar_tensor_tensor(
                            td_s[:, 0:gn * 512], qwd_st[par][:, 0:gn * 512], 1.0,
                            sd_st[par][:, 0:gn * 512], ALU.mult, ALU.mult).then_inc(sv5, 1)
                        vector.wait_ge(sv5, Bi + 1)
                        vector.tensor_add(wdc_s[:, g0 * 512:(g0 + gn) * 512],
                                          td_s[:, 0:gn * 512],
                                          nzd_st[par][:, 0:gn * 512]).then_inc(vdq5, 1)

            @blk.tensor
            def _(tensor):
                for c in range(8):
                    for q in range(NQ):
                        qi = c * NQ + q
                        par = qi % 2
                        if qi >= 2:
                            tensor.wait_ge(se5, 8 * (qi - 1))
                        for b in range(GB):
                            g0 = b * 4
                            gn = min(4, HT - g0)
                            bidx = qi * GB + b
                            bi = bidx % HPR
                            tensor.wait_ge(tmb, bidx)
                            tensor.wait_ge(dhp[bidx % 4], 16 * (bidx // 4 + 1))
                            tensor.wait_ge(vdq5, c * GB + b + 1)
                            for gg in range(gn):
                                g = g0 + gg
                                for tt in range(4):
                                    mm = tensor.matmul(
                                        dn_ps[par][:, tt * 512:(tt + 1) * 512],
                                        hp_s[bi][:, gg * 512 + tt * 128: gg * 512 + tt * 128 + 128],
                                        wdc_s[:, g * 512:(g + 1) * 512],
                                        start=(g == 0), stop=(g == HT - 1))
                                    if g == HT - 1 and tt < 3:
                                        mm.then_inc(tm5, 1)
                            mm.then_inc(tmb, 1)

            @blk.sync
            def _(sync):
                for c in range(8):
                    for q in range(NQ):
                        qi = c * NQ + q
                        for b in range(GB):
                            g0 = b * 4
                            gn = min(4, HT - g0)
                            bidx = qi * GB + b
                            bi = bidx % HPR
                            if bidx >= HPR:
                                sync.wait_ge(tmb, bidx - HPR + 1)
                            src = bass.AP(hT_d.tensor, (g0 * 128) * T + q * 512,
                                          [[T, 128], [128 * T, gn], [1, 512]])
                            sync.dma_start(hp_s[bi][:, 0:gn * 512].rearrange("p (g c2) -> p g c2", g=gn),
                                           src).then_inc(dhp[bidx % 4], 16)

            @blk.scalar
            def _(scalar):
                nst = [0] * 4
                for c in range(8):
                    for q in range(NQ):
                        qi = c * NQ + q
                        par = qi % 2
                        for tt in range(4):
                            gsi = qi * 4 + tt
                            if tt < 3:
                                scalar.wait_ge(tm5, 3 * qi + tt + 1)
                            else:
                                scalar.wait_ge(tmb, (qi + 1) * GB)
                            sti = gsi % 4
                            if nst[sti] > 0:
                                scalar.wait_ge(dfw[sti], 16 * nst[sti])
                            tg = q * 4 + tt
                            ins = scalar.activation(st_s[sti][:], dn_ps[par][:, tt * 512:(tt + 1) * 512],
                                                    AF.Copy,
                                                    accum_out=sump_s[:, tg * 8 + c:tg * 8 + c + 1])
                            ins.then_inc(se5, 1)
                            scalar.wait_ge(se5, 2 * gsi + 1)
                            scalar.dma_start(
                                bass.AP(fo_d.tensor, (tg * 128) * D + c * 512, [[D, 128], [1, 512]]),
                                st_s[sti][:]).then_inc(dfw[sti], 16)
                            ins = scalar.activation(sq5_s[:], st_s[sti][:], AF.Square,
                                                    accum_out=ssqp_s[:, tg * 8 + c:tg * 8 + c + 1])
                            ins.then_inc(se5, 1)
                            scalar.wait_ge(se5, 2 * gsi + 2)
                            nst[sti] += 1
                ctr = [28 * NT]
                for t in range(NT):
                    ins = scalar.activation(sq5_s[:, 0:8], sump_s[:, t * 8:(t + 1) * 8],
                                            AF.Copy, accum_out=sum_s[:])
                    ctr[0] += 1; ins.then_inc(sch, 1); scalar.wait_ge(sch, ctr[0])
                    ins = scalar.activation(sq5_s[:, 0:8], ssqp_s[:, t * 8:(t + 1) * 8],
                                            AF.Copy, accum_out=ssq_s[:])
                    ctr[0] += 1; ins.then_inc(sch, 1); scalar.wait_ge(sch, ctr[0])
                    ln_chain(scalar, ctr, sch, rstd4_s[:, t:t + 1], nmr4_s[:, t:t + 1])
                for i in range(4):
                    scalar.wait_ge(dfw[i], 16 * nst[i])
        p5.close()
        barrier(drains=[("tensor", tmb, 8 * NQ * GB), ("vector", vdq5, 8 * GB),
                        ("scalar", sch, 37 * NT)])

        if upto < 6:
            return nc
        df6 = [E(nc.semaphore(f"df6_{i}")) for i in range(2)]
        dx6 = [E(nc.semaphore(f"dx6_{i}")) for i in range(2)]
        s6 = E(nc.semaphore("s6"))
        v6 = E(nc.semaphore("v6"))
        dy6 = [E(nc.semaphore(f"dy6_{i}")) for i in range(2)]
        p6 = ExitStack()
        fo_s = [p6.enter_context(nc.sbuf_tensor(f"fo6{i}", [128, D], BF16)) for i in range(2)]
        x1f_s = [p6.enter_context(nc.sbuf_tensor(f"x1f{i}", [128, D], F32)) for i in range(2)]
        lnf_s = p6.enter_context(nc.sbuf_tensor("lnf_s", [128, D], F32))
        yo_s = [p6.enter_context(nc.sbuf_tensor(f"yo{i}", [128, D], F32)) for i in range(2)]
        with p6, nc.Block() as blk:
            @blk.sync
            def _(sync):
                for t in range(NT):
                    if t >= 2:
                        sync.wait_ge(v6, t - 1)
                    sync.dma_start(fo_s[t % 2][:], fo_d[t]).then_inc(df6[t % 2], 16)
                    sync.dma_start(x1f_s[t % 2][:], x1_d[t]).then_inc(dx6[t % 2], 16)

            @blk.scalar
            def _(scalar):
                for t in range(NT):
                    scalar.wait_ge(df6[t % 2], 16 * (t // 2 + 1))
                    if t >= 1:
                        scalar.wait_ge(v6, t)
                    scalar.activation(lnf_s[:], fo_s[t % 2][:], AF.Identity,
                                      bias=nmr4_s[:, t:t + 1],
                                      scale=rstd4_s[:, t:t + 1]).then_inc(s6, 1)

            @blk.vector
            def _(vector):
                for t in range(NT):
                    vector.wait_ge(s6, t + 1)
                    vector.wait_ge(dx6[t % 2], 16 * (t // 2 + 1))
                    if t >= 2:
                        vector.wait_ge(dy6[t % 2], 16 * (t // 2))
                    vector.tensor_add(yo_s[t % 2][:], lnf_s[:], x1f_s[t % 2][:]).then_inc(v6, 1)

            @blk.gpsimd
            def _(gpsimd):
                for t in range(NT):
                    gpsimd.wait_ge(v6, t + 1)
                    gpsimd.dma_start(y_d[t * 128:(t + 1) * 128, :], yo_s[t % 2][:]).then_inc(dy6[t % 2], 16)
                gpsimd.wait_ge(dy6[0], 16 * ((NT + 1) // 2))
                gpsimd.wait_ge(dy6[1], 16 * (NT // 2))

        if dump is not None:
            barrier()
            srcs = {"qkvT": (qkvT_d, NF, T, BF16), "x1": (x1_d, NT, D, F32),
                    "hT": (hT_d, HT, T, BF16), "fo": (fo_d, NT, D, BF16)}
            src_d, NTI, W, DT = srcs[dump]
            dbg_d = nc.dram_tensor("dbg", [NTI, 128, W], DT, kind="ExternalOutput").ap()
            ddmp = E(nc.semaphore("ddmp"))
            pdmp = ExitStack()
            dmp_s = pdmp.enter_context(nc.sbuf_tensor("dmp_s", [128, W], DT))
            with pdmp, nc.Block() as blk:
                @blk.gpsimd
                def _(gpsimd):
                    n = 0
                    for i in range(NTI):
                        gpsimd.dma_start(dmp_s[:], src_d[i]).then_inc(ddmp, 16)
                        n += 16
                        gpsimd.wait_ge(ddmp, n)
                        gpsimd.dma_start(dbg_d[i], dmp_s[:]).then_inc(ddmp, 16)
                        n += 16
                        gpsimd.wait_ge(ddmp, n)

    return nc


def make_consts():
    return {
        "ident": np.eye(128, dtype=ml_dtypes.bfloat16),
        "ones": np.ones((1, 128), dtype=ml_dtypes.bfloat16),
    }


def kernel(L0_attn_x, L0_attn_Wqkv, L0_attn_bqkv, L0_attn_Wo, L0_attn_bo,
           L0_ffn_qw_gate, L0_ffn_z_gate, L0_ffn_s_gate,
           L0_ffn_qw_up, L0_ffn_z_up, L0_ffn_s_up,
           L0_ffn_qw_down, L0_ffn_z_down, L0_ffn_s_down):
    x = np.ascontiguousarray(np.asarray(L0_attn_x), dtype=np.float32)
    B, T, D_ = x.shape
    nc = build(T=T, HT=np.asarray(L0_ffn_qw_gate).shape[1] // 128)
    common = {
        "wqkv": np.ascontiguousarray(L0_attn_Wqkv, dtype=np.float32),
        "bqkv": np.ascontiguousarray(L0_attn_bqkv, dtype=np.float32),
        "wo": np.ascontiguousarray(L0_attn_Wo, dtype=np.float32),
        "bo": np.ascontiguousarray(L0_attn_bo, dtype=np.float32),
        "qwg": np.ascontiguousarray(L0_ffn_qw_gate, dtype=np.int32),
        "zg": np.ascontiguousarray(L0_ffn_z_gate, dtype=np.int32),
        "sg": np.ascontiguousarray(L0_ffn_s_gate, dtype=np.float32),
        "qwu": np.ascontiguousarray(L0_ffn_qw_up, dtype=np.int32),
        "zu": np.ascontiguousarray(L0_ffn_z_up, dtype=np.int32),
        "su": np.ascontiguousarray(L0_ffn_s_up, dtype=np.float32),
        "qwd": np.ascontiguousarray(L0_ffn_qw_down, dtype=np.int32),
        "zd": np.ascontiguousarray(L0_ffn_z_down, dtype=np.int32),
        "sd": np.ascontiguousarray(L0_ffn_s_down, dtype=np.float32),
        **make_consts(),
    }
    in_maps = [dict(common, x=np.ascontiguousarray(x[c])) for c in range(B)]
    res = run_bass_kernel_spmd(nc, in_maps, core_ids=list(range(B)))
    return np.stack([np.asarray(r["y"], dtype=np.float32) for r in res.results], axis=0)


# version 05
# speedup vs baseline: 1.1977x; 1.1373x over previous
from contextlib import ExitStack

import numpy as np
import ml_dtypes

import concourse.bass as bass
import concourse.mybir as mybir
from concourse.bass_utils import run_bass_kernel_spmd

F32 = mybir.dt.float32
BF16 = mybir.dt.bfloat16
I32 = mybir.dt.int32
AF = mybir.ActivationFunctionType
ALU = mybir.AluOpType
AX = mybir.AxisListType.X

D = 4096
KD = D // 128
F = 3072
NF = F // 128
DK = 1024
NDK = DK // 128
GI = 32
EPS = 1e-5
SCL = 1.0 / 32.0


def build(T=2048, HT=86, upto=99, dump=None):
    H = 128 * HT
    NT = T // 128
    NC = T // 512
    NKT = T // 128
    nc = bass.Bass("TRN2", target_bir_lowering=False)

    x_d = nc.dram_tensor("x", [T, D], F32, kind="ExternalInput").ap()
    wqkv_d = nc.dram_tensor("wqkv", [D, F], F32, kind="ExternalInput").ap()
    bqkv_d = nc.dram_tensor("bqkv", [F], F32, kind="ExternalInput").ap()
    wo_d = nc.dram_tensor("wo", [DK, D], F32, kind="ExternalInput").ap()
    bo_d = nc.dram_tensor("bo", [D], F32, kind="ExternalInput").ap()
    qwg_d = nc.dram_tensor("qwg", [D, H], I32, kind="ExternalInput").ap()
    zg_d = nc.dram_tensor("zg", [GI, H], I32, kind="ExternalInput").ap()
    sg_d = nc.dram_tensor("sg", [GI, H], F32, kind="ExternalInput").ap()
    qwu_d = nc.dram_tensor("qwu", [D, H], I32, kind="ExternalInput").ap()
    zu_d = nc.dram_tensor("zu", [GI, H], I32, kind="ExternalInput").ap()
    su_d = nc.dram_tensor("su", [GI, H], F32, kind="ExternalInput").ap()
    qwd_d = nc.dram_tensor("qwd", [H, D], I32, kind="ExternalInput").ap()
    zd_d = nc.dram_tensor("zd", [HT, D], I32, kind="ExternalInput").ap()
    sd_d = nc.dram_tensor("sd", [HT, D], F32, kind="ExternalInput").ap()
    id_d = nc.dram_tensor("ident", [128, 128], BF16, kind="ExternalInput").ap()
    ones_d = nc.dram_tensor("ones", [1, 128], BF16, kind="ExternalInput").ap()
    y_d = nc.dram_tensor("y", [T, D], F32, kind="ExternalOutput").ap()

    qkvT_d = nc.dram_tensor("qkvT_v", [NF, 128, T], BF16, kind="Internal").ap()
    x1_d = nc.dram_tensor("x1_v", [NT, 128, D], F32, kind="Internal").ap()
    hT_d = nc.dram_tensor("hT_v", [HT, 128, T], BF16, kind="Internal").ap()
    fo_d = nc.dram_tensor("fo_v", [NT, 128, D], BF16, kind="Internal").ap()
    nzsg_d = nc.dram_tensor("nzsg_v", [GI, H], BF16, kind="Internal").ap()
    nzsu_d = nc.dram_tensor("nzsu_v", [GI, H], BF16, kind="Internal").ap()
    sgb_d = nc.dram_tensor("sgb_v", [GI, H], BF16, kind="Internal").ap()
    sub_d = nc.dram_tensor("sub_v", [GI, H], BF16, kind="Internal").ap()
    sgf_d = nc.dram_tensor("sgf_v", [HT, 4096], BF16, kind="Internal").ap()
    suf_d = nc.dram_tensor("suf_v", [HT, 4096], BF16, kind="Internal").ap()
    sdb_d = nc.dram_tensor("sdb_v", [HT, D], BF16, kind="Internal").ap()
    nzd_d = nc.dram_tensor("nzd_v", [HT, D], BF16, kind="Internal").ap()

    stack = ExitStack()
    E = stack.enter_context
    with stack:
        id_s = E(nc.sbuf_tensor("id_s", [128, 128], BF16))
        idf_s = E(nc.sbuf_tensor("idf_s", [128, 128], F32))
        ones_s = E(nc.sbuf_tensor("ones_s", [1, 128], BF16))
        bqT_s = E(nc.sbuf_tensor("bqT_s", [128, NF], F32))
        sum_s = E(nc.sbuf_tensor("sum_s", [128, 1], F32))
        ssq_s = E(nc.sbuf_tensor("ssq_s", [128, 1], F32))
        nmean_s = E(nc.sbuf_tensor("nmean_s", [128, 1], F32))
        ssqn_s = E(nc.sbuf_tensor("ssqn_s", [128, 1], F32))
        msq_s = E(nc.sbuf_tensor("msq_s", [128, 1], F32))
        var_s = E(nc.sbuf_tensor("var_s", [128, 1], F32))
        lnv_s = E(nc.sbuf_tensor("lnv_s", [128, 1], F32))
        rstd_s = E(nc.sbuf_tensor("rstd_s", [128, 1], F32))
        nmr_s = E(nc.sbuf_tensor("nmr_s", [128, 1], F32))
        sump_s = E(nc.sbuf_tensor("sump_s", [128, NT * 8], F32))
        ssqp_s = E(nc.sbuf_tensor("ssqp_s", [128, NT * 8], F32))
        rstd4_s = E(nc.sbuf_tensor("rstd4_s", [128, NT], F32))
        nmr4_s = E(nc.sbuf_tensor("nmr4_s", [128, NT], F32))

        bar = E(nc.semaphore("bar"))
        NBAR = [0]

        def barrier(drains=()):
            NBAR[0] += 1
            tgt = 5 * NBAR[0]
            dr = {}
            for (en, sem, val) in drains:
                dr.setdefault(en, []).append((sem, val))
            with nc.Block() as blk:
                for engname in ("tensor", "vector", "scalar", "gpsimd", "sync"):
                    def mk(eng, _t=tgt, _en=engname):
                        for (sem, val) in dr.get(_en, []):
                            eng.wait_ge(sem, val)
                        eng.nop().then_inc(bar, 1)
                        eng.wait_ge(bar, _t)
                    getattr(blk, engname)(mk)

        def ln_chain(scalar, ctr, sem, rstd_out, nmr_out):
            def step(ins):
                ctr[0] += 1
                ins.then_inc(sem, 1)
                scalar.wait_ge(sem, ctr[0])
            step(scalar.mul(nmean_s[:], sum_s[:], -1.0 / D))
            step(scalar.mul(ssqn_s[:], ssq_s[:], 1.0 / D))
            step(scalar.activation(msq_s[:], nmean_s[:], AF.Square))
            step(scalar.activation(var_s[:], msq_s[:], AF.Identity, scale=-1.0,
                                   bias=ssqn_s[:, 0:1]))
            step(scalar.activation(lnv_s[:], var_s[:], AF.Ln, bias=np.float32(EPS)))
            step(scalar.activation(rstd_out, lnv_s[:], AF.Exp, scale=-0.5))
            step(scalar.activation(nmr_out, nmean_s[:], AF.Identity, scale=rstd_out))

        jobs = []
        def chunks(total, size):
            out, off = [], 0
            while off < total:
                out.append((off, min(size, total - off)))
                off += size
            return out
        for (z, s, nz, sb, rows, total) in (
                (zg_d, sg_d, nzsg_d, sgb_d, GI, H),
                (zu_d, su_d, nzsu_d, sub_d, GI, H),
                (zd_d, sd_d, nzd_d, sdb_d, HT, D)):
            for (off, w) in chunks(total, 2816):
                jobs.append((z, s, nz, sb, rows, off, w))
        NJOB = len(jobs)
        dpp = [E(nc.semaphore(f"dpp{i}")) for i in range(2)]
        vpp = E(nc.semaphore("vpp"))
        dppo = [E(nc.semaphore(f"dppo{i}")) for i in range(2)]
        dflat = E(nc.semaphore("dflat"))
        ppS = ExitStack()
        ppz = [ppS.enter_context(nc.sbuf_tensor(f"ppz{i}", [128, 2816], I32)) for i in range(2)]
        pps = [ppS.enter_context(nc.sbuf_tensor(f"pps{i}", [128, 2816], F32)) for i in range(2)]
        ppn = [ppS.enter_context(nc.sbuf_tensor(f"ppn{i}", [128, 2816], BF16)) for i in range(2)]
        ppb = [ppS.enter_context(nc.sbuf_tensor(f"ppb{i}", [128, 2816], BF16)) for i in range(2)]
        ppf = ppS.enter_context(nc.sbuf_tensor("ppf", [GI, H], BF16))
        with ppS, nc.Block() as blk:
            @blk.sync
            def _(sync):
                for i, (z, s, nz, sb, rows, off, w) in enumerate(jobs):
                    if i >= 2:
                        sync.wait_ge(vpp, 2 * (i - 1))
                    sync.dma_start(ppz[i % 2][0:rows, 0:w], z[:, off:off + w]).then_inc(dpp[i % 2], 16)
                    sync.dma_start(pps[i % 2][0:rows, 0:w], s[:, off:off + w]).then_inc(dpp[i % 2], 16)

            @blk.vector
            def _(vector):
                for i, (z, s, nz, sb, rows, off, w) in enumerate(jobs):
                    vector.wait_ge(dpp[i % 2], 32 * (i // 2 + 1))
                    if i >= 2:
                        vector.wait_ge(dppo[i % 2], 32 * (i // 2))
                    vector.scalar_tensor_tensor(
                        ppn[i % 2][0:rows, 0:w], ppz[i % 2][0:rows, 0:w], -1.0, pps[i % 2][0:rows, 0:w],
                        ALU.mult, ALU.mult).then_inc(vpp, 1)
                    vector.tensor_copy(ppb[i % 2][0:rows, 0:w], pps[i % 2][0:rows, 0:w]).then_inc(vpp, 1)

            @blk.scalar
            def _(scalar):
                for i, (z, s, nz, sb, rows, off, w) in enumerate(jobs):
                    scalar.wait_ge(vpp, 2 * (i + 1))
                    scalar.dma_start(nz[:, off:off + w], ppn[i % 2][0:rows, 0:w]).then_inc(dppo[i % 2], 16)
                    scalar.dma_start(sb[:, off:off + w], ppb[i % 2][0:rows, 0:w]).then_inc(dppo[i % 2], 16)
                scalar.wait_ge(dppo[0], 32 * ((NJOB + 1) // 2))
                scalar.wait_ge(dppo[1], 32 * (NJOB // 2))
                scalar.nop().then_inc(vpp, 1)

            @blk.gpsimd
            def _(gpsimd):
                gpsimd.wait_ge(vpp, 2 * NJOB + 1)
                nfl = 0
                for (src, dst) in ((sgb_d, sgf_d), (sub_d, suf_d)):
                    gpsimd.dma_start(ppf[0:GI, :], src[:, :]).then_inc(dflat, 16)
                    nfl += 16
                    gpsimd.wait_ge(dflat, nfl)
                    dap = bass.AP(dst.tensor, 0, [[128, GI], [4096, HT], [1, 128]])
                    gpsimd.dma_start(
                        dap, ppf[0:GI, :].rearrange("p (j c) -> p j c", j=HT)
                    ).then_inc(dflat, 16)
                    nfl += 16
                    gpsimd.wait_ge(dflat, nfl)
        barrier()
        if upto < -4:
            return nc

        xnT_stk = ExitStack()
        xnT = xnT_stk.enter_context(nc.sbuf_tensor("xnT", [128, KD * T], BF16))
        dset = E(nc.semaphore("dset"))
        d0 = [E(nc.semaphore(f"d0_{i}")) for i in range(2)]
        lnS0 = E(nc.semaphore("lnS0"))
        sch = E(nc.semaphore("sch"))
        a0 = E(nc.semaphore("a0"))
        t0 = E(nc.semaphore("t0"))
        v0 = E(nc.semaphore("v0"))
        tbq = E(nc.semaphore("tbq"))
        vbq = E(nc.semaphore("vbq"))
        scp0 = E(nc.semaphore("scp0"))
        p0 = ExitStack()
        xr_s = [p0.enter_context(nc.sbuf_tensor(f"xr{i}", [128, D], F32)) for i in range(2)]
        xn_s = p0.enter_context(nc.sbuf_tensor("xn_s", [128, D], BF16))
        tr0_s = p0.enter_context(nc.sbuf_tensor("tr0_s", [128, D], BF16))
        tr0v_s = p0.enter_context(nc.sbuf_tensor("tr0v_s", [128, D], BF16))
        bq_stage = p0.enter_context(nc.sbuf_tensor("bq_stage", [NF, 128], F32))
        tp_ps = [p0.enter_context(nc.psum_tensor(f"tp{i}", [128, 1024], BF16)) for i in range(2)]
        bq_ps = p0.enter_context(nc.psum_tensor("bq_ps", [128, NF], F32))
        with p0, nc.Block() as blk:
            @blk.sync
            def _(sync):
                sync.dma_start(id_s[:], id_d).then_inc(dset, 16)
                sync.dma_start(ones_s[:], ones_d).then_inc(dset, 16)
                bq_src = bass.AP(bqkv_d.tensor, 0, [[128, NF], [1, 128]])
                sync.dma_start(bq_stage[:], bq_src).then_inc(dset, 16)
                for t in range(NT):
                    if t >= 2:
                        sync.wait_ge(a0, t - 1)
                    sync.dma_start(xr_s[t % 2][:], x_d[t * 128:(t + 1) * 128, :]).then_inc(d0[t % 2], 16)

            @blk.scalar
            def _(scalar):
                scalar.wait_ge(dset, 48)
                scalar.copy(idf_s[:], id_s[:]).then_inc(scp0, 1)
                ctr = [0]
                for t in range(NT):
                    scalar.wait_ge(d0[t % 2], 16 * (t // 2 + 1))
                    ins = scalar.activation(tr0_s[:], xr_s[t % 2][:], AF.Square,
                                            accum_out=ssq_s[:])
                    ctr[0] += 1; ins.then_inc(sch, 1); scalar.wait_ge(sch, ctr[0])
                    scalar.wait_ge(lnS0, t + 1)
                    ln_chain(scalar, ctr, sch, rstd_s[:, 0:1], nmr_s[:, 0:1])
                    if t >= 1:
                        scalar.wait_ge(t0, 4 * t)
                    ins = scalar.activation(xn_s[:], xr_s[t % 2][:], AF.Identity,
                                            bias=nmr_s[:, 0:1], scale=rstd_s[:, 0:1])
                    ctr[0] += 1; ins.then_inc(sch, 1)
                    scalar.wait_ge(sch, ctr[0])
                    scalar.nop().then_inc(a0, 1)

            @blk.vector
            def _(vector):
                vector.wait_ge(tbq, 1)
                vector.tensor_copy(bqT_s[:], bq_ps[:, :]).then_inc(vbq, 1)
                for t in range(NT):
                    vector.wait_ge(d0[t % 2], 16 * (t // 2 + 1))
                    vector.tensor_scalar(tr0v_s[:], xr_s[t % 2][:], 1.0, None,
                                         ALU.mult, ALU.add, accum_out=sum_s[:]).then_inc(lnS0, 1)
                    for gq in range(4):
                        vector.wait_ge(t0, 4 * t + gq + 1)
                        dst = xnT[:].rearrange("p (k tt) -> p k tt", k=KD)[
                            :, gq * 8:(gq + 1) * 8, t * 128:(t + 1) * 128]
                        vector.tensor_copy(
                            dst, tp_ps[gq % 2][:].rearrange("p (k c) -> p k c", k=8)
                        ).then_inc(v0, 1)

            @blk.tensor
            def _(tensor):
                tensor.wait_ge(scp0, 1)
                tensor.transpose(bq_ps[:, :], bq_stage[:, :], idf_s[0:NF, 0:NF]).then_inc(tbq, 1)
                for t in range(NT):
                    tensor.wait_ge(a0, t + 1)
                    for gq in range(4):
                        G = 4 * t + gq
                        if G >= 2:
                            tensor.wait_ge(v0, G - 1)
                        for k in range(8):
                            kk = gq * 8 + k
                            mm = tensor.transpose(
                                tp_ps[gq % 2][:, k * 128:(k + 1) * 128],
                                xn_s[:, kk * 128:(kk + 1) * 128], id_s[:, :])
                        mm.then_inc(t0, 1)
        barrier(drains=[("vector", v0, 4 * NT), ("vector", vbq, 1),
                        ("scalar", a0, NT), ("tensor", t0, 4 * NT)])
        if upto < -3:
            return nc

        dq1 = [E(nc.semaphore(f"dq1_{i}")) for i in range(2)]
        vc1 = E(nc.semaphore("vc1"))
        tm1 = E(nc.semaphore("tm1"))
        se1 = E(nc.semaphore("se1"))
        do1 = [E(nc.semaphore(f"do1_{i}")) for i in range(2)]
        p1 = ExitStack()
        wqf_s = [p1.enter_context(nc.sbuf_tensor(f"wqf{i}", [128, D], F32)) for i in range(2)]
        wqb_s = [p1.enter_context(nc.sbuf_tensor(f"wqb{i}", [128, D], BF16)) for i in range(2)]
        qv_s = [p1.enter_context(nc.sbuf_tensor(f"qv{i}", [128, T], BF16)) for i in range(2)]
        qk_ps = [p1.enter_context(nc.psum_tensor(f"qk{i}", [128, T], F32)) for i in range(2)]
        with p1, nc.Block() as blk:
            @blk.sync
            def _(sync):
                for j in range(NF):
                    if j >= 2:
                        sync.wait_ge(vc1, j - 1)
                    src = wqkv_d[:, j * 128:(j + 1) * 128].rearrange("(g p) c -> p g c", p=128)
                    dst = wqf_s[j % 2][:].rearrange("p (g c) -> p g c", g=KD)
                    sync.dma_start(dst, src).then_inc(dq1[j % 2], 16)

            @blk.vector
            def _(vector):
                for j in range(NF):
                    vector.wait_ge(dq1[j % 2], 16 * (j // 2 + 1))
                    if j >= 2:
                        vector.wait_ge(tm1, j - 1)
                    vector.tensor_copy(wqb_s[j % 2][:], wqf_s[j % 2][:]).then_inc(vc1, 1)

            @blk.tensor
            def _(tensor):
                for j in range(NF):
                    tensor.wait_ge(vc1, j + 1)
                    if j >= 2:
                        tensor.wait_ge(se1, j - 1)
                    for ch in range(NC):
                        for k in range(KD):
                            mm = tensor.matmul(
                                qk_ps[j % 2][:, ch * 512:(ch + 1) * 512],
                                wqb_s[j % 2][:, k * 128:(k + 1) * 128],
                                xnT[:, k * T + ch * 512: k * T + (ch + 1) * 512],
                                start=(k == 0), stop=(k == KD - 1))
                    mm.then_inc(tm1, 1)

            @blk.scalar
            def _(scalar):
                for j in range(NF):
                    scalar.wait_ge(tm1, j + 1)
                    if j >= 2:
                        scalar.wait_ge(do1[j % 2], 16 * (j // 2))
                    scalar.activation(qv_s[j % 2][:], qk_ps[j % 2][:, :], AF.Identity,
                                      bias=bqT_s[:, j:j + 1]).then_inc(se1, 1)
                    scalar.wait_ge(se1, j + 1)
                    scalar.dma_start(qkvT_d[j], qv_s[j % 2][:]).then_inc(do1[j % 2], 16)
                scalar.wait_ge(do1[0], 16 * ((NF + 1) // 2))
                scalar.wait_ge(do1[1], 16 * (NF // 2))
        p1.close()
        xnT_stk.close()
        barrier(drains=[("tensor", tm1, NF), ("vector", vc1, NF), ("scalar", se1, NF)])
        if upto < -2:
            return nc

        dkt = E(nc.semaphore("dkt"))
        dvt = [E(nc.semaphore(f"dvt{i}")) for i in range(2)]
        dwo = [E(nc.semaphore(f"dwo{i}")) for i in range(2)]
        dqt = [E(nc.semaphore(f"dqt{i}")) for i in range(2)]
        dxr = [E(nc.semaphore(f"dxr{i}")) for i in range(2)]
        tt2 = E(nc.semaphore("tt2"))
        vev2 = E(nc.semaphore("vev2"))
        vwoc = E(nc.semaphore("vwoc"))
        sboc = E(nc.semaphore("sboc"))
        smm = E(nc.semaphore("smm"))
        vmax2 = E(nc.semaphore("vmax2"))
        sexp = E(nc.semaphore("sexp"))
        vrec2 = E(nc.semaphore("vrec2"))
        tat = E(nc.semaphore("tat"))
        vat2 = E(nc.semaphore("vat2"))
        tav = E(nc.semaphore("tav"))
        sao = E(nc.semaphore("sao"))
        tot2 = E(nc.semaphore("tot2"))
        vaoT2 = E(nc.semaphore("vaoT2"))
        tpr = E(nc.semaphore("tpr"))
        pev = E(nc.semaphore("pev"))
        sln2 = E(nc.semaphore("sln2"))
        vx1 = E(nc.semaphore("vx1"))
        dx1w = [E(nc.semaphore(f"dx1w{i}")) for i in range(2)]
        dbo2 = E(nc.semaphore("dbo2"))
        p2 = ExitStack()
        kT_s = p2.enter_context(nc.sbuf_tensor("kT_s", [128, NDK * T], BF16))
        v_s = p2.enter_context(nc.sbuf_tensor("v_s", [128, NKT * DK], BF16))
        wo_s = p2.enter_context(nc.sbuf_tensor("wo_s", [128, NDK * D], BF16))
        qt_s = [p2.enter_context(nc.sbuf_tensor(f"qt{i}", [128, NDK * 128], BF16)) for i in range(2)]
        xr2_s = [p2.enter_context(nc.sbuf_tensor(f"xr2{i}", [128, D], F32)) for i in range(2)]
        a_s = p2.enter_context(nc.sbuf_tensor("a_s", [128, T], BF16))
        at_s = p2.enter_context(nc.sbuf_tensor("at_s", [128, T], BF16))
        ao_s = p2.enter_context(nc.sbuf_tensor("ao_s", [128, DK], BF16))
        aoT_s = p2.enter_context(nc.sbuf_tensor("aoT_s", [128, DK], BF16))
        ap_s = p2.enter_context(nc.sbuf_tensor("ap_s", [128, D], BF16))
        tr2_s = p2.enter_context(nc.sbuf_tensor("tr2_s", [128, D], BF16))
        bo_s = p2.enter_context(nc.sbuf_tensor("bo_s", [1, D], BF16))
        nmx_s = p2.enter_context(nc.sbuf_tensor("nmx_s", [128, 1], F32))
        nmx32_s = p2.enter_context(nc.sbuf_tensor("nmx32_s", [128, 1], F32))
        sume_s = p2.enter_context(nc.sbuf_tensor("sume_s", [128, 1], F32))
        rec_s = p2.enter_context(nc.sbuf_tensor("rec_s", [128, 1], F32))
        sup_s = p2.enter_context(nc.sbuf_tensor("sup_s", [128, 8], F32))
        s_ps = p2.enter_context(nc.psum_tensor("s_ps", [128, max(T, 2048)], F32))
        at_ps = p2.enter_context(nc.psum_tensor("at_ps", [128, max(T, DK)], BF16))
        av_ps = p2.enter_context(nc.psum_tensor("av_ps", [128, DK], F32))
        with p2, nc.Block() as blk:
            @blk.sync
            def _(sync):
                src = bass.AP(qkvT_d.tensor, NDK * 128 * T,
                              [[T, 128], [128 * T, NDK], [1, T]])
                sync.dma_start(kT_s[:].rearrange("p (j tt) -> p j tt", j=NDK), src).then_inc(dkt, 16)
                sync.dma_start(xr2_s[0][0:1, :], bass.AP(bo_d.tensor, 0, [[D, 1], [1, D]])).then_inc(dbo2, 16)
                for j in range(NDK):
                    if j >= 2:
                        sync.wait_ge(tt2, j - 1)
                    vstage = a_s if j % 2 == 0 else at_s
                    sync.dma_start(vstage[:], qkvT_d[2 * NDK + j]).then_inc(dvt[j % 2], 16)
                for j in range(NDK):
                    if j == 0:
                        sync.wait_ge(sboc, 1)
                    if j >= 2:
                        sync.wait_ge(vwoc, j - 1)
                    sync.dma_start(xr2_s[j % 2][:], wo_d[j * 128:(j + 1) * 128, :]).then_inc(dwo[j % 2], 16)
                for t in range(NT):
                    if t >= 2:
                        sync.wait_ge(smm, t - 1)
                        sync.wait_ge(dx1w[t % 2], 16 * (t // 2))
                    if t < 2:
                        sync.wait_ge(vwoc, NDK)
                    src = bass.AP(qkvT_d.tensor, t * 128, [[T, 128], [128 * T, NDK], [1, 128]])
                    sync.dma_start(qt_s[t % 2][:].rearrange("p (j c) -> p j c", j=NDK), src).then_inc(dqt[t % 2], 16)
                    sync.dma_start(xr2_s[t % 2][:], x_d[t * 128:(t + 1) * 128, :]).then_inc(dxr[t % 2], 16)

            @blk.tensor
            def _(tensor):
                for j in range(NDK):
                    tensor.wait_ge(dvt[j % 2], 16 * (j // 2 + 1))
                    if j >= 1:
                        tensor.wait_ge(vev2, j)
                    vstage = a_s if j % 2 == 0 else at_s
                    for kt in range(NKT):
                        mm = tensor.transpose(at_ps[:, kt * 128:(kt + 1) * 128],
                                              vstage[:, kt * 128:(kt + 1) * 128], id_s[:, :])
                    mm.then_inc(tt2, 1)
                for t in range(NT):
                    tensor.wait_ge(dqt[t % 2], 16 * (t // 2 + 1))
                    if t == 0:
                        tensor.wait_ge(dkt, 16)
                        tensor.wait_ge(vev2, NDK)
                    if t >= 1:
                        tensor.wait_ge(sexp, t)
                        tensor.wait_ge(pev, 8 * t)
                    for ch in range(NC):
                        for j in range(NDK):
                            mm = tensor.matmul(
                                s_ps[:, ch * 512:(ch + 1) * 512],
                                qt_s[t % 2][:, j * 128:(j + 1) * 128],
                                kT_s[:, j * T + ch * 512: j * T + (ch + 1) * 512],
                                start=(j == 0), stop=(j == NDK - 1))
                    mm.then_inc(smm, 1)
                    tensor.wait_ge(sexp, t + 1)
                    if t >= 1:
                        tensor.wait_ge(vaoT2, t)
                    for kt in range(NKT):
                        mm = tensor.transpose(at_ps[:, kt * 128:(kt + 1) * 128],
                                              a_s[:, kt * 128:(kt + 1) * 128], id_s[:, :])
                    mm.then_inc(tat, 1)
                    tensor.wait_ge(vat2, t + 1)
                    if t >= 1:
                        tensor.wait_ge(sao, t)
                    for ch in range(2):
                        for kt in range(NKT):
                            mm = tensor.matmul(
                                av_ps[:, ch * 512:(ch + 1) * 512],
                                at_s[:, kt * 128:(kt + 1) * 128],
                                v_s[:, kt * DK + ch * 512: kt * DK + (ch + 1) * 512],
                                start=(kt == 0), stop=(kt == NKT - 1))
                    mm.then_inc(tav, 1)
                    tensor.wait_ge(sao, t + 1)
                    for j in range(NDK):
                        mm = tensor.transpose(at_ps[:, j * 128:(j + 1) * 128],
                                              ao_s[:, j * 128:(j + 1) * 128], id_s[:, :])
                    mm.then_inc(tot2, 1)
                    tensor.wait_ge(vaoT2, t + 1)
                    if t == 0:
                        tensor.wait_ge(vwoc, NDK)
                        tensor.wait_ge(sboc, 1)
                    for c in range(8):
                        if c >= 4:
                            tensor.wait_ge(pev, 8 * t + c - 3)
                        ps = s_ps[:, (c % 4) * 512:((c % 4) + 1) * 512]
                        mm = tensor.matmul(ps, ones_s[:, :], bo_s[:, c * 512:(c + 1) * 512],
                                           start=True, stop=False)
                        for j in range(NDK):
                            mm = tensor.matmul(ps, aoT_s[:, j * 128:(j + 1) * 128],
                                               wo_s[:, j * D + c * 512: j * D + (c + 1) * 512],
                                               start=False, stop=(j == NDK - 1))
                        mm.then_inc(tpr, 1)

            @blk.vector
            def _(vector):
                for j in range(NDK):
                    vector.wait_ge(tt2, j + 1)
                    dst = v_s[:].rearrange("p (kt c) -> p kt c", kt=NKT)[:, :, j * 128:(j + 1) * 128]
                    vector.tensor_copy(dst, at_ps[:, 0:T].rearrange("p (kt c) -> p kt c", kt=NKT)).then_inc(vev2, 1)
                for j in range(NDK):
                    vector.wait_ge(dwo[j % 2], 16 * (j // 2 + 1))
                    vector.tensor_copy(wo_s[:, j * D:(j + 1) * D], xr2_s[j % 2][:]).then_inc(vwoc, 1)
                for t in range(NT):
                    vector.wait_ge(smm, t + 1)
                    if t >= 1:
                        vector.wait_ge(sexp, t)
                    vector.reduce_max(nmx_s[:], s_ps[:, 0:T], axis=AX, negate=True).then_inc(vmax2, 1)
                    vector.wait_ge(sexp, t + 1)
                    vector.reciprocal(rec_s[:], sume_s[:]).then_inc(vrec2, 1)
                    vector.wait_ge(tat, t + 1)
                    vector.tensor_copy(at_s[:], at_ps[:, 0:T]).then_inc(vat2, 1)
                    vector.wait_ge(tot2, t + 1)
                    if t >= 1:
                        vector.wait_ge(tpr, 8 * t)
                    vector.tensor_copy(aoT_s[:], at_ps[:, 0:DK]).then_inc(vaoT2, 1)
                    vector.wait_ge(sln2, t + 1)
                    vector.wait_ge(dxr[t % 2], 16 * (t // 2 + 1))
                    vector.tensor_add(xr2_s[t % 2][:], ap_s[:], xr2_s[t % 2][:]).then_inc(vx1, 1)

            @blk.scalar
            def _(scalar):
                scalar.wait_ge(dbo2, 16)
                scalar.copy(bo_s[:], xr2_s[0][0:1, :]).then_inc(sboc, 1)
                ctr = [9 * NT]
                def step(ins):
                    ctr[0] += 1; ins.then_inc(sch, 1); scalar.wait_ge(sch, ctr[0])
                for t in range(NT):
                    scalar.wait_ge(vmax2, t + 1)
                    if t >= 1:
                        scalar.wait_ge(tat, t)
                        scalar.wait_ge(vrec2, t)
                    step(scalar.mul(nmx32_s[:], nmx_s[:], SCL))
                    scalar.activation(a_s[:], s_ps[:, 0:T], AF.Exp, bias=nmx32_s[:, 0:1],
                                      scale=SCL, accum_out=sume_s[:]).then_inc(sexp, 1)
                    scalar.wait_ge(tav, t + 1)
                    scalar.wait_ge(vrec2, t + 1)
                    scalar.activation(ao_s[:], av_ps[:, :], AF.Copy,
                                      scale=rec_s[:, 0:1]).then_inc(sao, 1)
                    if t >= 1:
                        scalar.wait_ge(vx1, t)
                    for c in range(8):
                        scalar.wait_ge(tpr, 8 * t + c + 1)
                        scalar.activation(ap_s[:, c * 512:(c + 1) * 512],
                                          s_ps[:, (c % 4) * 512:((c % 4) + 1) * 512],
                                          AF.Copy, accum_out=sup_s[:, c:c + 1]).then_inc(pev, 1)
                    scalar.wait_ge(pev, 8 * (t + 1))
                    step(scalar.activation(tr2_s[:, 0:8], sup_s[:], AF.Copy,
                                           accum_out=sum_s[:]))
                    step(scalar.activation(tr2_s[:], ap_s[:], AF.Square,
                                           accum_out=ssq_s[:]))
                    ln_chain(scalar, ctr, sch, rstd_s[:, 0:1], nmr_s[:, 0:1])
                    ins = scalar.activation(ap_s[:], ap_s[:], AF.Identity, bias=nmr_s[:, 0:1],
                                            scale=rstd_s[:, 0:1])
                    ins.then_inc(sln2, 1)
                    scalar.wait_ge(vx1, t + 1)
                    scalar.dma_start(x1_d[t], xr2_s[t % 2][:]).then_inc(dx1w[t % 2], 16)
                scalar.wait_ge(dx1w[0], 16 * ((NT + 1) // 2))
                scalar.wait_ge(dx1w[1], 16 * (NT // 2))
        p2.close()
        barrier(drains=[("tensor", tpr, 8 * NT), ("vector", vx1, NT),
                        ("scalar", sln2, NT)])
        if upto < -1:
            return nc

        xn2T_stk = ExitStack()
        xn2T = xn2T_stk.enter_context(nc.sbuf_tensor("xn2T", [128, KD * T], BF16))
        xg2T_s = xn2T_stk.enter_context(nc.sbuf_tensor("xg2T", [32, T], BF16))
        sxg3 = E(nc.semaphore("sxg3"))
        vxg3 = E(nc.semaphore("vxg3"))
        p3 = ExitStack()
        x1r_s = [p3.enter_context(nc.sbuf_tensor(f"x1r{i}", [128, D], F32)) for i in range(2)]
        xn2_s = p3.enter_context(nc.sbuf_tensor("xn2_s", [128, D], BF16))
        tr3_s = p3.enter_context(nc.sbuf_tensor("tr3_s", [128, D], BF16))
        tr3v_s = p3.enter_context(nc.sbuf_tensor("tr3v_s", [128, D], BF16))
        xg2f_s = p3.enter_context(nc.sbuf_tensor("xg2f", [128, 32], F32))
        xg2b_s = p3.enter_context(nc.sbuf_tensor("xg2b", [128, 32], BF16))
        tp3_ps = [p3.enter_context(nc.psum_tensor(f"tp3{i}", [128, 1024], BF16)) for i in range(2)]
        xg2_ps = p3.enter_context(nc.psum_tensor("xg2_ps", [32, 128], BF16))
        with p3, nc.Block() as blk:
            @blk.sync
            def _(sync):
                for t in range(NT):
                    if t >= 2:
                        sync.wait_ge(a0, NT + t - 1)
                    sync.dma_start(x1r_s[t % 2][:], x1_d[t]).then_inc(dq1[t % 2], 16)

            @blk.scalar
            def _(scalar):
                ctr = [19 * NT]
                for t in range(NT):
                    scalar.wait_ge(dq1[t % 2], 16 * (NF // 2) + 16 * (t // 2 + 1))
                    ins = scalar.activation(tr3_s[:], x1r_s[t % 2][:], AF.Square,
                                            accum_out=ssq_s[:])
                    ctr[0] += 1; ins.then_inc(sch, 1); scalar.wait_ge(sch, ctr[0])
                    scalar.wait_ge(lnS0, NT + t + 1)
                    ln_chain(scalar, ctr, sch, rstd_s[:, 0:1], nmr_s[:, 0:1])
                    if t >= 1:
                        scalar.wait_ge(t0, 4 * NT + 4 * t)
                        scalar.wait_ge(vxg3, t)
                    ins = scalar.activation(xn2_s[:], x1r_s[t % 2][:], AF.Identity,
                                            bias=nmr_s[:, 0:1], scale=rstd_s[:, 0:1])
                    ctr[0] += 1; ins.then_inc(sch, 1)
                    scalar.wait_ge(sch, ctr[0])
                    scalar.nop().then_inc(a0, 1)
                    scalar.wait_ge(vxg3, t + 1)
                    if t >= 1:
                        scalar.wait_ge(tbq, 1 + t)
                    scalar.copy(xg2b_s[:], xg2f_s[:]).then_inc(sxg3, 1)

            @blk.vector
            def _(vector):
                for t in range(NT):
                    vector.wait_ge(dq1[t % 2], 16 * (NF // 2) + 16 * (t // 2 + 1))
                    vector.tensor_scalar(tr3v_s[:], x1r_s[t % 2][:], 1.0, None,
                                         ALU.mult, ALU.add, accum_out=sum_s[:]).then_inc(lnS0, 1)
                    vector.wait_ge(a0, NT + t + 1)
                    vector.reduce_sum(xg2f_s[:], xn2_s[:].rearrange("p (g c) -> p g c", g=32),
                                      axis=AX).then_inc(vxg3, 1)
                    for gq in range(4):
                        vector.wait_ge(t0, 4 * NT + 4 * t + gq + 1)
                        dst = xn2T[:].rearrange("p (k tt) -> p k tt", k=KD)[
                            :, gq * 8:(gq + 1) * 8, t * 128:(t + 1) * 128]
                        vector.tensor_copy(
                            dst, tp3_ps[gq % 2][:].rearrange("p (k c) -> p k c", k=8)
                        ).then_inc(v0, 1)
                    vector.wait_ge(tbq, 1 + t + 1)
                    vector.tensor_copy(xg2T_s[:, t * 128:(t + 1) * 128], xg2_ps[:, :]).then_inc(vbq, 1)

            @blk.tensor
            def _(tensor):
                for t in range(NT):
                    tensor.wait_ge(a0, NT + t + 1)
                    for gq in range(4):
                        G = 4 * t + gq
                        if G >= 2:
                            tensor.wait_ge(v0, 4 * NT + G - 1)
                        for k in range(8):
                            kk = gq * 8 + k
                            mm = tensor.transpose(tp3_ps[gq % 2][:, k * 128:(k + 1) * 128],
                                                  xn2_s[:, kk * 128:(kk + 1) * 128], id_s[:, :])
                        mm.then_inc(t0, 1)
                    tensor.wait_ge(sxg3, t + 1)
                    if t >= 1:
                        tensor.wait_ge(vbq, 1 + t)
                    tensor.transpose(xg2_ps[:, :], xg2b_s[:, :], id_s[:, :]).then_inc(tbq, 1)
        p3.close()
        barrier(drains=[("vector", v0, 8 * NT), ("vector", vbq, 1 + NT),
                        ("tensor", tbq, 1 + NT), ("scalar", sxg3, NT)])
        if upto < 0:
            return nc

        QWR = 3
        SR = 3
        WR = 6
        dqg = [E(nc.semaphore(f"dqg{i}")) for i in range(QWR)]
        dqu = [E(nc.semaphore(f"dqu{i}")) for i in range(QWR)]
        dgn = [E(nc.semaphore(f"dgn{i}")) for i in range(2)]
        dun = [E(nc.semaphore(f"dun{i}")) for i in range(2)]
        vwg = E(nc.semaphore("vwg"))
        vwu = E(nc.semaphore("vwu"))
        tgq = E(nc.semaphore("tgq"))
        tuq = E(nc.semaphore("tuq"))
        ssi4 = E(nc.semaphore("ssi4"))
        vh4 = E(nc.semaphore("vh4"))
        dhw = E(nc.semaphore("dhw"))
        p4 = ExitStack()
        sgq_s = [p4.enter_context(nc.sbuf_tensor(f"sgq{i}", [128, 1024], BF16)) for i in range(SR)]
        suq_s = [p4.enter_context(nc.sbuf_tensor(f"suq{i}", [128, 1024], BF16)) for i in range(SR)]
        qwgq_s = [p4.enter_context(nc.sbuf_tensor(f"qwgq{i}", [128, 1024], I32)) for i in range(QWR)]
        qwuq_s = [p4.enter_context(nc.sbuf_tensor(f"qwuq{i}", [128, 1024], I32)) for i in range(QWR)]
        nzg_s = [p4.enter_context(nc.sbuf_tensor(f"nzg{i}", [32, 128], BF16)) for i in range(2)]
        nzu_s = [p4.enter_context(nc.sbuf_tensor(f"nzu{i}", [32, 128], BF16)) for i in range(2)]
        wpg_s = [p4.enter_context(nc.sbuf_tensor(f"wpg{i}", [128, 1024], BF16)) for i in range(WR)]
        wpu_s = [p4.enter_context(nc.sbuf_tensor(f"wpu{i}", [128, 1024], BF16)) for i in range(WR)]
        sg_sb = p4.enter_context(nc.sbuf_tensor("sg_sb", [128, T], BF16))
        h_s = p4.enter_context(nc.sbuf_tensor("h4", [128, T], BF16))
        gps_g = p4.enter_context(nc.psum_tensor("gps_g", [128, T], F32))
        gps_u = p4.enter_context(nc.psum_tensor("gps_u", [128, T], F32))
        with p4, nc.Block() as blk:
            @blk.sync
            def _(sync):
                for j in range(HT):
                    par = j % 2
                    if j >= 2:
                        sync.wait_ge(tgq, 4 * (j - 1))
                    sync.dma_start(nzg_s[par][:], nzsg_d[:, j * 128:(j + 1) * 128]).then_inc(dgn[par], 16)
                    for q in range(4):
                        Q = 4 * j + q
                        if Q >= QWR:
                            sync.wait_ge(vwg, Q - QWR + 1)
                        src = bass.AP(qwg_d.tensor, (q * 8 * 128) * H + j * 128,
                                      [[H, 128], [128 * H, 8], [1, 128]])
                        sync.dma_start(qwgq_s[Q % QWR][:].rearrange("p (g c) -> p g c", g=8),
                                       src).then_inc(dqg[Q % QWR], 16)
                        bsrc = bass.AP(sgf_d.tensor, j * 4096 + q * 1024, [[0, 128], [1, 1024]])
                        sync.dma_start(sgq_s[Q % SR][:], bsrc).then_inc(dqg[Q % QWR], 16)

            @blk.gpsimd
            def _(gpsimd):
                for j in range(HT):
                    par = j % 2
                    if j >= 2:
                        gpsimd.wait_ge(tuq, 4 * (j - 1))
                    gpsimd.dma_start(nzu_s[par][:], nzsu_d[:, j * 128:(j + 1) * 128]).then_inc(dun[par], 16)
                    for q in range(4):
                        Q = 4 * j + q
                        if Q >= QWR:
                            gpsimd.wait_ge(vwu, Q - QWR + 1)
                        src = bass.AP(qwu_d.tensor, (q * 8 * 128) * H + j * 128,
                                      [[H, 128], [128 * H, 8], [1, 128]])
                        gpsimd.dma_start(qwuq_s[Q % QWR][:].rearrange("p (g c) -> p g c", g=8),
                                         src).then_inc(dqu[Q % QWR], 16)
                        bsrc = bass.AP(suf_d.tensor, j * 4096 + q * 1024, [[0, 128], [1, 1024]])
                        gpsimd.dma_start(suq_s[Q % SR][:], bsrc).then_inc(dqu[Q % QWR], 16)

            @blk.vector
            def _(vector):
                for j in range(HT):
                    for q in range(4):
                        Q = 4 * j + q
                        vector.wait_ge(dqg[Q % QWR], 32 * (Q // QWR + 1))
                        if Q >= WR:
                            vector.wait_ge(tgq, Q - WR + 1)
                        vector.scalar_tensor_tensor(
                            wpg_s[Q % WR][:], qwgq_s[Q % QWR][:], 1.0,
                            sgq_s[Q % SR][:], ALU.mult, ALU.mult).then_inc(vwg, 1)
                    for q in range(4):
                        Q = 4 * j + q
                        vector.wait_ge(dqu[Q % QWR], 32 * (Q // QWR + 1))
                        if Q >= WR:
                            vector.wait_ge(tuq, Q - WR + 1)
                        vector.scalar_tensor_tensor(
                            wpu_s[Q % WR][:], qwuq_s[Q % QWR][:], 1.0,
                            suq_s[Q % SR][:], ALU.mult, ALU.mult).then_inc(vwu, 1)
                    vector.wait_ge(tuq, 4 * (j + 1))
                    vector.wait_ge(ssi4, j + 1)
                    if j >= 1:
                        vector.wait_ge(dhw, 16 * j)
                    vector.tensor_tensor(h_s[:], gps_u[:, :], sg_sb[:],
                                         ALU.mult).then_inc(vh4, 1)

            @blk.tensor
            def _(tensor):
                for j in range(HT):
                    par = j % 2
                    tensor.wait_ge(dgn[par], 16 * (j // 2 + 1))
                    if j >= 1:
                        tensor.wait_ge(ssi4, j)
                    for ch in range(NC):
                        tensor.matmul(gps_g[:, ch * 512:(ch + 1) * 512], nzg_s[par][:, :],
                                      xg2T_s[:, ch * 512:(ch + 1) * 512],
                                      start=True, stop=False)
                    for k in range(KD):
                        Q = 4 * j + k // 8
                        if k % 8 == 0:
                            tensor.wait_ge(vwg, Q + 1)
                        w = wpg_s[Q % WR][:, (k % 8) * 128:(k % 8 + 1) * 128]
                        for ch in range(NC):
                            mm = tensor.matmul(gps_g[:, ch * 512:(ch + 1) * 512], w,
                                               xn2T[:, k * T + ch * 512: k * T + (ch + 1) * 512],
                                               start=False, stop=(k == KD - 1))
                        if k % 8 == 7:
                            mm.then_inc(tgq, 1)
                    tensor.wait_ge(dun[par], 16 * (j // 2 + 1))
                    if j >= 1:
                        tensor.wait_ge(vh4, j)
                    for ch in range(NC):
                        tensor.matmul(gps_u[:, ch * 512:(ch + 1) * 512], nzu_s[par][:, :],
                                      xg2T_s[:, ch * 512:(ch + 1) * 512],
                                      start=True, stop=False)
                    for k in range(KD):
                        Q = 4 * j + k // 8
                        if k % 8 == 0:
                            tensor.wait_ge(vwu, Q + 1)
                        w = wpu_s[Q % WR][:, (k % 8) * 128:(k % 8 + 1) * 128]
                        for ch in range(NC):
                            mm = tensor.matmul(gps_u[:, ch * 512:(ch + 1) * 512], w,
                                               xn2T[:, k * T + ch * 512: k * T + (ch + 1) * 512],
                                               start=False, stop=(k == KD - 1))
                        if k % 8 == 7:
                            mm.then_inc(tuq, 1)

            @blk.scalar
            def _(scalar):
                for j in range(HT):
                    scalar.wait_ge(tgq, 4 * (j + 1))
                    if j >= 1:
                        scalar.wait_ge(vh4, j)
                    scalar.activation(sg_sb[:], gps_g[:, :], AF.Silu).then_inc(ssi4, 1)
                    scalar.wait_ge(vh4, j + 1)
                    scalar.dma_start(hT_d[j], h_s[:]).then_inc(dhw, 16)
                scalar.wait_ge(dhw, 16 * HT)
        p4.close()
        xn2T_stk.close()
        barrier(drains=[("tensor", tuq, 4 * HT), ("vector", vh4, HT), ("scalar", ssi4, HT)])

        if upto < 5:
            return nc
        NQ = NT // 4
        GB = (HT + 3) // 4
        DQR = 4
        HPR = 6
        dqd = [E(nc.semaphore(f"dqd{i}")) for i in range(DQR)]
        vdq5 = E(nc.semaphore("vdq5"))
        sv5 = E(nc.semaphore("sv5"))
        tmb = E(nc.semaphore("tmb"))
        tm5 = E(nc.semaphore("tm5"))
        se5 = E(nc.semaphore("se5"))
        dhp = [E(nc.semaphore(f"dhp{i}")) for i in range(4)]
        dfw = [E(nc.semaphore(f"dfw{i}")) for i in range(4)]
        p5 = ExitStack()
        wdc_s = p5.enter_context(nc.sbuf_tensor("wdc", [128, HT * 512], BF16))
        hp_s = [p5.enter_context(nc.sbuf_tensor(f"hp{i}", [128, 4 * 512], BF16)) for i in range(HPR)]
        st_s = [p5.enter_context(nc.sbuf_tensor(f"st{i}", [128, 512], BF16)) for i in range(4)]
        qwd_st = [p5.enter_context(nc.sbuf_tensor(f"qwdst{i}", [128, 2048], I32)) for i in range(DQR)]
        sd_st = [p5.enter_context(nc.sbuf_tensor(f"sdst{i}", [128, 2048], BF16)) for i in range(DQR)]
        nzd_st = [p5.enter_context(nc.sbuf_tensor(f"nzdst{i}", [128, 2048], BF16)) for i in range(DQR)]
        td_s = p5.enter_context(nc.sbuf_tensor("td5", [128, 2048], BF16))
        sq5_s = p5.enter_context(nc.sbuf_tensor("sq5", [128, 512], BF16))
        dn_ps = [p5.enter_context(nc.psum_tensor(f"dn{i}", [128, 2048], F32)) for i in range(2)]
        with p5, nc.Block() as blk:
            @blk.gpsimd
            def _(gpsimd):
                for c in range(8):
                    for b in range(GB):
                        Bi = c * GB + b
                        par = Bi % DQR
                        g0 = b * 4
                        gn = min(4, HT - g0)
                        if Bi >= DQR:
                            gpsimd.wait_ge(vdq5, Bi - DQR + 1)
                        src = bass.AP(qwd_d.tensor, (g0 * 128) * D + c * 512,
                                      [[D, 128], [128 * D, gn], [1, 512]])
                        gpsimd.dma_start(qwd_st[par][:, 0:gn * 512].rearrange("p (g c2) -> p g c2", g=gn),
                                         src).then_inc(dqd[par], 16)
                        src = bass.AP(sdb_d.tensor, g0 * D + c * 512, [[0, 128], [D, gn], [1, 512]])
                        gpsimd.dma_start(sd_st[par][:, 0:gn * 512].rearrange("p (g c2) -> p g c2", g=gn),
                                         src).then_inc(dqd[par], 16)
                        src = bass.AP(nzd_d.tensor, g0 * D + c * 512, [[0, 128], [D, gn], [1, 512]])
                        gpsimd.dma_start(nzd_st[par][:, 0:gn * 512].rearrange("p (g c2) -> p g c2", g=gn),
                                         src).then_inc(dqd[par], 16)

            @blk.vector
            def _(vector):
                for c in range(8):
                    for b in range(GB):
                        Bi = c * GB + b
                        par = Bi % DQR
                        g0 = b * 4
                        gn = min(4, HT - g0)
                        vector.wait_ge(dqd[par], 48 * (Bi // DQR + 1))
                        if Bi >= 1:
                            vector.wait_ge(vdq5, Bi)
                        if c >= 1:
                            vector.wait_ge(tmb, (c * NQ - 1) * GB + b + 1)
                        vector.scalar_tensor_tensor(
                            td_s[:, 0:gn * 512], qwd_st[par][:, 0:gn * 512], 1.0,
                            sd_st[par][:, 0:gn * 512], ALU.mult, ALU.mult).then_inc(sv5, 1)
                        vector.wait_ge(sv5, Bi + 1)
                        vector.tensor_add(wdc_s[:, g0 * 512:(g0 + gn) * 512],
                                          td_s[:, 0:gn * 512],
                                          nzd_st[par][:, 0:gn * 512]).then_inc(vdq5, 1)

            @blk.tensor
            def _(tensor):
                for c in range(8):
                    for q in range(NQ):
                        qi = c * NQ + q
                        par = qi % 2
                        if qi >= 2:
                            tensor.wait_ge(se5, 8 * (qi - 1))
                        for b in range(GB):
                            g0 = b * 4
                            gn = min(4, HT - g0)
                            bidx = qi * GB + b
                            bi = bidx % HPR
                            tensor.wait_ge(tmb, bidx)
                            tensor.wait_ge(dhp[bidx % 4], 16 * (bidx // 4 + 1))
                            tensor.wait_ge(vdq5, c * GB + b + 1)
                            for gg in range(gn):
                                g = g0 + gg
                                for tt in range(4):
                                    mm = tensor.matmul(
                                        dn_ps[par][:, tt * 512:(tt + 1) * 512],
                                        hp_s[bi][:, gg * 512 + tt * 128: gg * 512 + tt * 128 + 128],
                                        wdc_s[:, g * 512:(g + 1) * 512],
                                        start=(g == 0), stop=(g == HT - 1))
                                    if g == HT - 1 and tt < 3:
                                        mm.then_inc(tm5, 1)
                            mm.then_inc(tmb, 1)

            @blk.sync
            def _(sync):
                for c in range(8):
                    for q in range(NQ):
                        qi = c * NQ + q
                        for b in range(GB):
                            g0 = b * 4
                            gn = min(4, HT - g0)
                            bidx = qi * GB + b
                            bi = bidx % HPR
                            if bidx >= HPR:
                                sync.wait_ge(tmb, bidx - HPR + 1)
                            src = bass.AP(hT_d.tensor, (g0 * 128) * T + q * 512,
                                          [[T, 128], [128 * T, gn], [1, 512]])
                            sync.dma_start(hp_s[bi][:, 0:gn * 512].rearrange("p (g c2) -> p g c2", g=gn),
                                           src).then_inc(dhp[bidx % 4], 16)

            @blk.scalar
            def _(scalar):
                nst = [0] * 4
                for c in range(8):
                    for q in range(NQ):
                        qi = c * NQ + q
                        par = qi % 2
                        for tt in range(4):
                            gsi = qi * 4 + tt
                            if tt < 3:
                                scalar.wait_ge(tm5, 3 * qi + tt + 1)
                            else:
                                scalar.wait_ge(tmb, (qi + 1) * GB)
                            sti = gsi % 4
                            if nst[sti] > 0:
                                scalar.wait_ge(dfw[sti], 16 * nst[sti])
                            tg = q * 4 + tt
                            ins = scalar.activation(st_s[sti][:], dn_ps[par][:, tt * 512:(tt + 1) * 512],
                                                    AF.Copy,
                                                    accum_out=sump_s[:, tg * 8 + c:tg * 8 + c + 1])
                            ins.then_inc(se5, 1)
                            scalar.wait_ge(se5, 2 * gsi + 1)
                            scalar.dma_start(
                                bass.AP(fo_d.tensor, (tg * 128) * D + c * 512, [[D, 128], [1, 512]]),
                                st_s[sti][:]).then_inc(dfw[sti], 16)
                            ins = scalar.activation(sq5_s[:], st_s[sti][:], AF.Square,
                                                    accum_out=ssqp_s[:, tg * 8 + c:tg * 8 + c + 1])
                            ins.then_inc(se5, 1)
                            scalar.wait_ge(se5, 2 * gsi + 2)
                            nst[sti] += 1
                ctr = [28 * NT]
                for t in range(NT):
                    ins = scalar.activation(sq5_s[:, 0:8], sump_s[:, t * 8:(t + 1) * 8],
                                            AF.Copy, accum_out=sum_s[:])
                    ctr[0] += 1; ins.then_inc(sch, 1); scalar.wait_ge(sch, ctr[0])
                    ins = scalar.activation(sq5_s[:, 0:8], ssqp_s[:, t * 8:(t + 1) * 8],
                                            AF.Copy, accum_out=ssq_s[:])
                    ctr[0] += 1; ins.then_inc(sch, 1); scalar.wait_ge(sch, ctr[0])
                    ln_chain(scalar, ctr, sch, rstd4_s[:, t:t + 1], nmr4_s[:, t:t + 1])
                for i in range(4):
                    scalar.wait_ge(dfw[i], 16 * nst[i])
        p5.close()
        barrier(drains=[("tensor", tmb, 8 * NQ * GB), ("vector", vdq5, 8 * GB),
                        ("scalar", sch, 37 * NT)])

        if upto < 6:
            return nc
        df6 = [E(nc.semaphore(f"df6_{i}")) for i in range(2)]
        dx6 = [E(nc.semaphore(f"dx6_{i}")) for i in range(2)]
        s6 = E(nc.semaphore("s6"))
        v6 = E(nc.semaphore("v6"))
        dy6 = [E(nc.semaphore(f"dy6_{i}")) for i in range(2)]
        p6 = ExitStack()
        fo_s = [p6.enter_context(nc.sbuf_tensor(f"fo6{i}", [128, D], BF16)) for i in range(2)]
        x1f_s = [p6.enter_context(nc.sbuf_tensor(f"x1f{i}", [128, D], F32)) for i in range(2)]
        lnf_s = p6.enter_context(nc.sbuf_tensor("lnf_s", [128, D], F32))
        yo_s = [p6.enter_context(nc.sbuf_tensor(f"yo{i}", [128, D], F32)) for i in range(2)]
        with p6, nc.Block() as blk:
            @blk.sync
            def _(sync):
                for t in range(NT):
                    if t >= 2:
                        sync.wait_ge(v6, t - 1)
                    sync.dma_start(fo_s[t % 2][:], fo_d[t]).then_inc(df6[t % 2], 16)
                    sync.dma_start(x1f_s[t % 2][:], x1_d[t]).then_inc(dx6[t % 2], 16)

            @blk.scalar
            def _(scalar):
                for t in range(NT):
                    scalar.wait_ge(df6[t % 2], 16 * (t // 2 + 1))
                    if t >= 1:
                        scalar.wait_ge(v6, t)
                    scalar.activation(lnf_s[:], fo_s[t % 2][:], AF.Identity,
                                      bias=nmr4_s[:, t:t + 1],
                                      scale=rstd4_s[:, t:t + 1]).then_inc(s6, 1)

            @blk.vector
            def _(vector):
                for t in range(NT):
                    vector.wait_ge(s6, t + 1)
                    vector.wait_ge(dx6[t % 2], 16 * (t // 2 + 1))
                    if t >= 2:
                        vector.wait_ge(dy6[t % 2], 16 * (t // 2))
                    vector.tensor_add(yo_s[t % 2][:], lnf_s[:], x1f_s[t % 2][:]).then_inc(v6, 1)

            @blk.gpsimd
            def _(gpsimd):
                for t in range(NT):
                    gpsimd.wait_ge(v6, t + 1)
                    gpsimd.dma_start(y_d[t * 128:(t + 1) * 128, :], yo_s[t % 2][:]).then_inc(dy6[t % 2], 16)
                gpsimd.wait_ge(dy6[0], 16 * ((NT + 1) // 2))
                gpsimd.wait_ge(dy6[1], 16 * (NT // 2))

        if dump is not None:
            barrier()
            srcs = {"qkvT": (qkvT_d, NF, T, BF16), "x1": (x1_d, NT, D, F32),
                    "hT": (hT_d, HT, T, BF16), "fo": (fo_d, NT, D, BF16)}
            src_d, NTI, W, DT = srcs[dump]
            dbg_d = nc.dram_tensor("dbg", [NTI, 128, W], DT, kind="ExternalOutput").ap()
            ddmp = E(nc.semaphore("ddmp"))
            pdmp = ExitStack()
            dmp_s = pdmp.enter_context(nc.sbuf_tensor("dmp_s", [128, W], DT))
            with pdmp, nc.Block() as blk:
                @blk.gpsimd
                def _(gpsimd):
                    n = 0
                    for i in range(NTI):
                        gpsimd.dma_start(dmp_s[:], src_d[i]).then_inc(ddmp, 16)
                        n += 16
                        gpsimd.wait_ge(ddmp, n)
                        gpsimd.dma_start(dbg_d[i], dmp_s[:]).then_inc(ddmp, 16)
                        n += 16
                        gpsimd.wait_ge(ddmp, n)

    return nc


def make_consts():
    return {
        "ident": np.eye(128, dtype=ml_dtypes.bfloat16),
        "ones": np.ones((1, 128), dtype=ml_dtypes.bfloat16),
    }


def kernel(L0_attn_x, L0_attn_Wqkv, L0_attn_bqkv, L0_attn_Wo, L0_attn_bo,
           L0_ffn_qw_gate, L0_ffn_z_gate, L0_ffn_s_gate,
           L0_ffn_qw_up, L0_ffn_z_up, L0_ffn_s_up,
           L0_ffn_qw_down, L0_ffn_z_down, L0_ffn_s_down):
    x = np.ascontiguousarray(np.asarray(L0_attn_x), dtype=np.float32)
    B, T, D_ = x.shape
    nc = build(T=T, HT=np.asarray(L0_ffn_qw_gate).shape[1] // 128)
    common = {
        "wqkv": np.ascontiguousarray(L0_attn_Wqkv, dtype=np.float32),
        "bqkv": np.ascontiguousarray(L0_attn_bqkv, dtype=np.float32),
        "wo": np.ascontiguousarray(L0_attn_Wo, dtype=np.float32),
        "bo": np.ascontiguousarray(L0_attn_bo, dtype=np.float32),
        "qwg": np.ascontiguousarray(L0_ffn_qw_gate, dtype=np.int32),
        "zg": np.ascontiguousarray(L0_ffn_z_gate, dtype=np.int32),
        "sg": np.ascontiguousarray(L0_ffn_s_gate, dtype=np.float32),
        "qwu": np.ascontiguousarray(L0_ffn_qw_up, dtype=np.int32),
        "zu": np.ascontiguousarray(L0_ffn_z_up, dtype=np.int32),
        "su": np.ascontiguousarray(L0_ffn_s_up, dtype=np.float32),
        "qwd": np.ascontiguousarray(L0_ffn_qw_down, dtype=np.int32),
        "zd": np.ascontiguousarray(L0_ffn_z_down, dtype=np.int32),
        "sd": np.ascontiguousarray(L0_ffn_s_down, dtype=np.float32),
        **make_consts(),
    }
    in_maps = [dict(common, x=np.ascontiguousarray(x[c])) for c in range(B)]
    res = run_bass_kernel_spmd(nc, in_maps, core_ids=list(range(B)))
    return np.stack([np.asarray(r["y"], dtype=np.float32) for r in res.results], axis=0)
